# version 03
# baseline (speedup 1.0000x reference)
import math
import numpy as np
from contextlib import ExitStack

import concourse.bass as bass
import concourse.mybir as mybir
from concourse.bass_utils import run_bass_kernel_spmd

F32 = mybir.dt.float32
BF16 = mybir.dt.bfloat16
AF = mybir.ActivationFunctionType
OP = mybir.AluOpType
AX = mybir.AxisListType

D = 1024
T = 8192
NB = 8
NCORE = 8
KT = D // 128
CH = 512
NCHUNK = T // CH

ENGS = ("tensor", "vector", "scalar", "gpsimd", "sync")


class Prog:

    def __init__(self, nc):
        self.nc = nc
        self.st = ExitStack()
        self.clk = self.st.enter_context(nc.semaphore("clk"))
        self.dma_sem = self.st.enter_context(nc.semaphore("dmasem"))
        self.clk_total = 0
        self.dma_total = 0
        self.active = {}
        self.step_dmas = 0
        self.wait_tgt = (0, 0)
        self.eng_waited = {e: (0, 0) for e in ENGS}

    def eng(self, name):
        return getattr(self.nc, name)

    def _pre(self, name):
        w = self.eng_waited[name]
        tgt = self.wait_tgt
        e = self.eng(name)
        if w[0] < tgt[0]:
            e.wait_ge(self.clk, tgt[0])
        if w[1] < tgt[1]:
            e.wait_ge(self.dma_sem, 16 * tgt[1])
        self.eng_waited[name] = tgt

    def emit(self, name, fn):
        self._pre(name)
        inst = fn(self.eng(name))
        self.active[name] = inst
        return inst

    def dma(self, q, out, in_):
        self._pre(q)
        inst = self.eng(q).dma_start(out=out, in_=in_)
        inst.then_inc(self.dma_sem, 16)
        self.step_dmas += 1
        if q not in self.active:
            self.active[q] = None
        return inst

    def step(self):
        for name, inst in self.active.items():
            if inst is not None:
                inst.then_inc(self.clk, 1)
                self.clk_total += 1
        self.dma_total += self.step_dmas
        self.active = {}
        self.step_dmas = 0
        self.wait_tgt = (self.clk_total, self.dma_total)

    def finish(self):
        self.step()
        e = self.eng("sync")
        e.wait_ge(self.clk, self.clk_total)
        e.wait_ge(self.dma_sem, 16 * self.dma_total)

    def mm(self, out, lhsT, rhs, start, stop):
        return self.emit("tensor", lambda e: e.matmul(out, lhsT, rhs, start=start, stop=stop))

    def transpose(self, out, in_, ident):
        return self.emit("tensor", lambda e: e.transpose(out, in_, ident))

    def tt(self, out, a, b, op, eng="vector"):
        return self.emit(eng, lambda e: e.tensor_tensor(out=out, in0=a, in1=b, op=op))

    def ts(self, out, a, s1, op, eng="vector"):
        return self.emit(eng, lambda e: e.tensor_scalar(out=out, in0=a, scalar1=s1, scalar2=None, op0=op))

    def ts2(self, out, a, s1, s2, op0, op1, eng="vector"):
        return self.emit(eng, lambda e: e.tensor_scalar(out=out, in0=a, scalar1=s1, scalar2=s2, op0=op0, op1=op1))

    def act(self, out, in_, func, bias=None, scale=None):
        kw = {}
        if bias is not None:
            kw["bias"] = bias
        if scale is not None:
            kw["scale"] = scale
        return self.emit("scalar", lambda e: e.activation(out, in_, func, **kw))

    def copy(self, out, in_, eng="scalar"):
        if eng == "scalar":
            return self.emit(eng, lambda e: e.copy(out, in_))
        return self.emit(eng, lambda e: e.tensor_copy(out=out, in_=in_))

    def red(self, out, in_, op, negate=False, eng="vector"):
        return self.emit(eng, lambda e: e.tensor_reduce(out=out, in_=in_, axis=AX.X, op=op, negate=negate))

    def scan_cumsum(self, out, ones, data, init):
        return self.emit("vector", lambda e: e.tensor_tensor_scan(
            out=out, data0=ones, data1=data, initial=init, op0=OP.mult, op1=OP.add))

    def memset(self, ap, c, eng="vector"):
        return self.emit(eng, lambda e: e.memset(ap, c))

    def scope(self):
        from contextlib import contextmanager

        @contextmanager
        def _s():
            old = self.st
            self.st = ExitStack()
            try:
                yield
            finally:
                self.st.close()
                self.st = old
        return _s()

    def sb(self, name, shape, dtype):
        self._n = getattr(self, "_n", 0) + 1
        return self.st.enter_context(self.nc.sbuf_tensor(f"sb{self._n}_{name}", shape, dtype))

    def ps(self, name, shape, dtype=F32):
        self._n = getattr(self, "_n", 0) + 1
        return self.st.enter_context(self.nc.psum_tensor(f"ps{self._n}_{name}", shape, dtype))


def _mk_nc(name, ins, outs):
    nc = bass.Bass("TRN2", target_bir_lowering=False)
    aps = {}
    for n, (shp, dt) in ins.items():
        aps[n] = nc.dram_tensor(n, list(shp), dt, kind="ExternalInput").ap()
    for n, (shp, dt) in outs.items():
        aps[n] = nc.dram_tensor(n, list(shp), dt, kind="ExternalOutput").ap()
    return nc, aps



def make_consts():
    import ml_dtypes
    C = {}
    d = np.arange(D, dtype=np.float64) / (D - 1)
    th = 0.001 ** d
    s = np.arange(T, dtype=np.float64)
    ph = th[:, None] * s[None, :]
    eneg = np.exp(-1e-5 * s)[None, :]
    epos = np.exp(1e-5 * s)[None, :]
    C["tabP"] = (eneg * np.cos(ph)).astype(np.float32)
    C["tabQ"] = (eneg * np.sin(ph)).astype(np.float32)
    C["tabU"] = (epos * np.cos(ph)).astype(np.float32)
    C["tabV"] = (epos * np.sin(ph)).astype(np.float32)
    lam = np.exp(1e-5)
    C["lam_a"] = (lam * np.cos(th)).astype(np.float32).reshape(D, 1)
    C["lam_b"] = (lam * np.sin(th)).astype(np.float32).reshape(D, 1)
    hi = np.arange(0, 64, 2, dtype=np.float64)
    invf = 1.0 / 10000.0 ** (hi / 64.0)
    t = np.arange(T, dtype=np.float64)
    fr = np.concatenate([t[:, None] * invf, t[:, None] * invf], 1)
    base = (hi + 0.4 * 64) / (1.4 * 64)
    power = (t - T // 2) / 256.0
    scl = np.concatenate([base[None, :] ** power[:, None]] * 2, 1)
    cos, sin = np.cos(fr), np.sin(fr)
    mk = lambda a: np.vstack([a.T, a.T]).astype(np.float32)
    C["tab_qc"] = mk(cos * scl * 8.0)
    C["tab_qs"] = mk(sin * scl * 8.0)
    C["tab_kc"] = mk(cos / scl)
    C["tab_ks"] = mk(sin / scl)
    jj = np.arange(512)[:, None]
    ii = np.arange(512)[None, :]
    C["maskTprev"] = np.where(jj >= ii, 0.0, -1e30).astype(np.float32)
    C["maskTcur"] = np.where(jj <= ii, 0.0, -1e30).astype(np.float32)
    r = np.arange(128)
    C["maskblk"] = np.where((r[:, None] // 8) == (r[None, :] // 8), 0.0, -1e30).astype(np.float32)
    P64 = np.zeros((64, 64))
    for rr in range(32):
        P64[rr, rr + 32] = -1.0
        P64[rr + 32, rr] = 1.0
    Perm = np.zeros((128, 128))
    Perm[:64, :64] = P64
    Perm[64:, 64:] = P64
    C["permT"] = Perm.T.astype(ml_dtypes.bfloat16)
    C["identbf"] = np.eye(128).astype(ml_dtypes.bfloat16)
    C["identf"] = np.eye(128, dtype=np.float32)
    return C



def load_wbf(P, ap, kt, mtot, stage, tiles):
    for k in range(kt):
        P.dma("sync", stage[:, 0:mtot], ap[k * 128:(k + 1) * 128, :])
        P.step()
        P.copy(tiles[k][:, 0:mtot], stage[:, 0:mtot])
        P.step()


def load_pvec(P, ap, tiles):
    for k in range(len(tiles)):
        P.dma("sync", tiles[k][:, :], ap[k * 128:(k + 1) * 128, :])
    P.step()


def ln_rows(P, tiles, n, sq, psum, ones_col, ones_row, rstd_row, nm_row,
            rep_a, rep_b, eps, pr):
    for k in range(8):
        P.emit("scalar", lambda e, k=k: e.activation(sq[:, :n], tiles[k][:, :n], AF.Square))
        P.step()
        P.mm(psum[0:1, 0:n], ones_col[:, :], tiles[k][:, :n], start=(k == 0), stop=(k == 7))
        P.mm(pr[0:1, 0:n], ones_col[:, :], sq[:, :n], start=(k == 0), stop=(k == 7))
        P.step()
    P.ts(rstd_row[0:1, :n], pr[0:1, 0:n], 1.0 / D, OP.mult)
    P.ts(nm_row[0:1, :n], psum[0:1, 0:n], 1.0 / D, OP.mult)
    P.step()
    P.tt(pr[0:1, 0:n], nm_row[0:1, :n], nm_row[0:1, :n], OP.mult)
    P.step()
    P.tt(rstd_row[0:1, :n], rstd_row[0:1, :n], pr[0:1, 0:n], OP.subtract)
    P.step()
    P.ts(rstd_row[0:1, :n], rstd_row[0:1, :n], float(eps), OP.add)
    P.step()
    P.act(rstd_row[0:1, :n], rstd_row[0:1, :n], AF.Ln)
    P.step()
    P.act(rstd_row[0:1, :n], rstd_row[0:1, :n], AF.Exp, scale=-0.5)
    P.step()
    P.tt(nm_row[0:1, :n], nm_row[0:1, :n], rstd_row[0:1, :n], OP.mult)
    P.step()
    P.ts(nm_row[0:1, :n], nm_row[0:1, :n], -1.0, OP.mult)
    P.step()
    for off in range(0, n, 512):
        w = min(512, n - off)
        P.mm(psum[:, 0:w], ones_row[0:1, :], rstd_row[0:1, off:off + w], start=True, stop=True)
        P.mm(pr[:, 0:w], ones_row[0:1, :], nm_row[0:1, off:off + w], start=True, stop=True)
        P.step()
        P.copy(rep_a[:, off:off + w], psum[:, 0:w], eng="vector")
        P.copy(rep_b[:, off:off + w], pr[:, 0:w])
        P.step()



def stage_l0h(P, a):
  with P.scope():
      CHL = 512
      wz = [P.sb(f"wz{k}", [128, D], BF16) for k in range(8)]
      wza = [P.sb(f"wza{k}", [128, D], BF16) for k in range(8)]
      stage = P.sb("stage", [128, D], F32)
      bz_t = [P.sb(f"bz{k}", [128, 1], F32) for k in range(8)]
      bza_t = [P.sb(f"bza{k}", [128, 1], F32) for k in range(8)]
      cc = [P.sb(f"cc{k}", [128, 1], F32) for k in range(8)]
      cs = [P.sb(f"cs{k}", [128, 1], F32) for k in range(8)]
      tmpc = [P.sb(f"tc{k}", [128, 1], F32) for k in range(2)]
      ones_col = P.sb("onescol", [128, 1], F32)
      ones_row = P.sb("onesrow", [1, 128], F32)
      ones_ch = P.sb("onesch", [128, CHL], F32)
      x_t = [P.sb(f"x{k}", [128, CHL], F32) for k in range(8)]
      xnbf = [P.sb(f"xnbf{k}", [128, CHL], BF16) for k in range(8)]
      z_t = [P.sb(f"z{k}", [128, CHL], F32) for k in range(8)]
      tab1 = [P.sb(f"tb1{k}", [128, CHL], F32) for k in range(8)]
      tab2 = [P.sb(f"tb2{k}", [128, CHL], F32) for k in range(8)]
      Sc = [P.sb(f"scn{k}", [128, CHL], F32) for k in range(8)]
      Ss = [P.sb(f"ssn{k}", [128, CHL], F32) for k in range(8)]
      tmp = P.sb("tmp", [128, CHL], F32)
      sq = P.sb("sq", [128, CHL], F32)
      rep_a = P.sb("repa", [128, CHL], F32)
      rep_b = P.sb("repb", [128, CHL], F32)
      rstd_row = P.sb("rstd", [1, CHL], F32)
      nm_row = P.sb("nm", [1, CHL], F32)
      pa = P.ps("pa", [128, 512])
      pb = P.ps("pb", [128, 512])
      pr = P.ps("pr", [128, 512])

      P.memset(ones_col[:, :], 1.0)
      P.memset(ones_row[:, :], 1.0)
      P.memset(ones_ch[:, :], 1.0)
      load_pvec(P, a["bz"], bz_t)
      load_pvec(P, a["bza"], bza_t)
      for k in range(8):
          sl = slice(k * 128, (k + 1) * 128)
          P.dma("sync", cc[k][:, :], a["lc_re"][sl, :])
          P.dma("sync", cs[k][:, :], a["lc_im"][sl, :])
          P.dma("sync", tmpc[0][:, :], a["lam_a"][sl, :])
          P.dma("sync", tmpc[1][:, :], a["lam_b"][sl, :])
          P.step()
          P.tt(sq[:, 0:1], cc[k][:, :], tmpc[0][:, :], OP.mult)
          P.tt(tmp[:, 0:1], cs[k][:, :], tmpc[1][:, :], OP.mult, eng="gpsimd")
          P.step()
          P.tt(sq[:, 1:2], cc[k][:, :], tmpc[1][:, :], OP.mult)
          P.tt(tmp[:, 1:2], cs[k][:, :], tmpc[0][:, :], OP.mult, eng="gpsimd")
          P.step()
          P.tt(cc[k][:, :], sq[:, 0:1], tmp[:, 0:1], OP.subtract)
          P.step()
          P.tt(cs[k][:, :], sq[:, 1:2], tmp[:, 1:2], OP.add)
          P.step()
          P.ts(cs[k][:, :], cs[k][:, :], -1.0, OP.mult)
          P.step()
      load_wbf(P, a["WzT"], 8, D, stage, wz)
      load_wbf(P, a["WzaT"], 8, D, stage, wza)

      for c in range(T // CHL):
          c0 = c * CHL
          sl = slice(c0, c0 + CHL)
          for k in range(8):
              P.dma("sync", x_t[k][:, :], a["xT"][k * 128:(k + 1) * 128, sl])
              P.dma("gpsimd", tab1[k][:, :], a["tabP"][k * 128:(k + 1) * 128, sl])
              P.dma("gpsimd", tab2[k][:, :], a["tabQ"][k * 128:(k + 1) * 128, sl])
          P.step()
          ln_rows(P, x_t, CHL, sq, pa, ones_col, ones_row, rstd_row, nm_row,
                  rep_a, rep_b, 1e-5, pr)
          for k in range(8):
              P.tt(x_t[k][:, :], x_t[k][:, :], rep_a[:, :], OP.mult)
              P.step()
              P.tt(x_t[k][:, :], x_t[k][:, :], rep_b[:, :], OP.add)
              P.step()
              P.copy(xnbf[k][:, :], x_t[k][:, :])
              P.dma("sync", a["xnT"][k * 128:(k + 1) * 128, sl], x_t[k][:, :])
              P.step()
          for o in range(8):
              for k in range(8):
                  P.mm(pa[:, :], wz[k][:, o * 128:(o + 1) * 128], xnbf[k][:, :],
                       start=(k == 0), stop=(k == 7))
                  P.mm(pb[:, :], wza[k][:, o * 128:(o + 1) * 128], xnbf[k][:, :],
                       start=(k == 0), stop=(k == 7))
              P.step()
              P.act(tmp[:, :], pb[:, :], AF.Silu, bias=bza_t[o][:, 0:1])
              P.step()
              P.ts(z_t[o][:, :], pa[:, :], bz_t[o][:, 0:1], OP.add)
              P.step()
              P.tt(z_t[o][:, :], z_t[o][:, :], tmp[:, :], OP.mult)
              P.step()
          for k in range(8):
              P.tt(tmp[:, :], z_t[k][:, :], tab1[k][:, :], OP.mult)
              P.step()
              P.scan_cumsum(Sc[k][:, :], ones_ch[:, :], tmp[:, :], cc[k][:, 0:1])
              P.step()
              P.copy(cc[k][:, 0:1], Sc[k][:, CHL - 1:CHL], eng="vector")
              P.tt(tmp[:, :], z_t[k][:, :], tab2[k][:, :], OP.mult, eng="gpsimd")
              P.step()
              P.scan_cumsum(Ss[k][:, :], ones_ch[:, :], tmp[:, :], cs[k][:, 0:1])
              P.step()
              P.copy(cs[k][:, 0:1], Ss[k][:, CHL - 1:CHL], eng="vector")
              P.step()
          for k in range(8):
              P.dma("sync", tab1[k][:, :], a["tabU"][k * 128:(k + 1) * 128, sl])
              P.dma("gpsimd", tab2[k][:, :], a["tabV"][k * 128:(k + 1) * 128, sl])
          P.step()
          for k in range(8):
              P.tt(Sc[k][:, :], Sc[k][:, :], tab1[k][:, :], OP.mult)
              P.tt(Ss[k][:, :], Ss[k][:, :], tab2[k][:, :], OP.mult, eng="gpsimd")
              P.step()
              P.tt(Sc[k][:, :], Sc[k][:, :], Ss[k][:, :], OP.add)
              P.step()
              P.dma("sync", a["hT"][k * 128:(k + 1) * 128, sl], Sc[k][:, :])
          P.step()


def build_l0h():
    nc, a = _mk_nc(
        "l0h",
        ins={"xT": ((D, T), F32), "WzT": ((D, D), F32), "WzaT": ((D, D), F32),
             "bz": ((D, 1), F32), "bza": ((D, 1), F32),
             "lc_re": ((D, 1), F32), "lc_im": ((D, 1), F32),
             "lam_a": ((D, 1), F32), "lam_b": ((D, 1), F32),
             "tabP": ((D, T), F32), "tabQ": ((D, T), F32),
             "tabU": ((D, T), F32), "tabV": ((D, T), F32)},
        outs={"xnT": ((D, T), F32), "hT": ((D, T), F32)},
    )
    P = Prog(nc)
    stage_l0h(P, a)
    P.finish()
    return nc


def stage_l0y(P, a):
  with P.scope():
      CHL = 512
      wy = [P.sb(f"wy{k}", [128, D], BF16) for k in range(8)]
      wya = [P.sb(f"wya{k}", [128, D], BF16) for k in range(8)]
      stage = P.sb("stage", [128, D], F32)
      by_t = [P.sb(f"by{k}", [128, 1], F32) for k in range(8)]
      bya_t = [P.sb(f"bya{k}", [128, 1], F32) for k in range(8)]
      nw_t = [P.sb(f"nw{k}", [128, 1], F32) for k in range(8)]
      ones_col = P.sb("onescol", [128, 1], F32)
      ones_row = P.sb("onesrow", [1, 128], F32)
      h_t = [P.sb(f"h{k}", [128, CHL], F32) for k in range(8)]
      nbf = [P.sb(f"nbf{k}", [128, CHL], BF16) for k in range(8)]
      xbf = [P.sb(f"xbf{k}", [128, CHL], BF16) for k in range(8)]
      sq = P.sb("sq", [128, CHL], F32)
      tmp = P.sb("tmp", [128, CHL], F32)
      sout = P.sb("sout", [128, CHL], F32)
      rep_a = P.sb("repa", [128, CHL], F32)
      rrow = P.sb("rrow", [1, CHL], F32)
      pa = P.ps("pa", [128, 512])
      pb = P.ps("pb", [128, 512])

      P.memset(ones_col[:, :], 1.0)
      P.memset(ones_row[:, :], 1.0)
      load_pvec(P, a["by"], by_t)
      load_pvec(P, a["bya"], bya_t)
      load_pvec(P, a["norm_w"], nw_t)
      load_wbf(P, a["WyT"], 8, D, stage, wy)
      load_wbf(P, a["WyaT"], 8, D, stage, wya)

      for c in range(T // CHL):
          sl = slice(c * CHL, c * CHL + CHL)
          for k in range(8):
              P.dma("sync", h_t[k][:, :], a["hT"][k * 128:(k + 1) * 128, sl])
          P.step()
          for k in range(8):
              P.emit("scalar", lambda e, k=k: e.activation(sq[:, :], h_t[k][:, :], AF.Square))
              P.step()
              P.mm(pa[0:1, :], ones_col[:, :], sq[:, :], start=(k == 0), stop=(k == 7))
              P.step()
          P.ts2(rrow[0:1, :], pa[0:1, :], 1.0 / D, 1e-6, OP.mult, OP.add)
          P.step()
          P.act(rrow[0:1, :], rrow[0:1, :], AF.Ln)
          P.step()
          P.act(rrow[0:1, :], rrow[0:1, :], AF.Exp, scale=-0.5)
          P.step()
          P.mm(pa[:, :], ones_row[0:1, :], rrow[0:1, :], start=True, stop=True)
          P.step()
          P.copy(rep_a[:, :], pa[:, :], eng="vector")
          P.step()
          for k in range(8):
              P.tt(tmp[:, :], h_t[k][:, :], rep_a[:, :], OP.mult)
              P.step()
              P.ts(tmp[:, :], tmp[:, :], nw_t[k][:, 0:1], OP.mult)
              P.step()
              P.copy(nbf[k][:, :], tmp[:, :])
              P.step()
          for k in range(8):
              P.dma("sync", h_t[k][:, :], a["xnT"][k * 128:(k + 1) * 128, sl])
          P.step()
          for k in range(8):
              P.copy(xbf[k][:, :], h_t[k][:, :])
          P.step()
          for o in range(8):
              for k in range(8):
                  P.mm(pa[:, :], wy[k][:, o * 128:(o + 1) * 128], nbf[k][:, :],
                       start=(k == 0), stop=(k == 7))
                  P.mm(pb[:, :], wya[k][:, o * 128:(o + 1) * 128], xbf[k][:, :],
                       start=(k == 0), stop=(k == 7))
              P.step()
              P.act(tmp[:, :], pb[:, :], AF.Silu, bias=bya_t[o][:, 0:1])
              P.step()
              P.ts(sout[:, :], pa[:, :], by_t[o][:, 0:1], OP.add)
              P.step()
              P.tt(sout[:, :], sout[:, :], tmp[:, :], OP.mult)
              P.step()
              P.dma("sync", a["sT"][o * 128:(o + 1) * 128, sl], sout[:, :])
              P.step()


def build_l0y():
    nc, a = _mk_nc(
        "l0y",
        ins={"hT": ((D, T), F32), "xnT": ((D, T), F32),
             "WyT": ((D, D), F32), "WyaT": ((D, D), F32),
             "by": ((D, 1), F32), "bya": ((D, 1), F32), "norm_w": ((D, 1), F32)},
        outs={"sT": ((D, T), F32)},
    )
    P = Prog(nc)
    stage_l0y(P, a)
    P.finish()
    return nc


def stage_l0moe(P, a):
  with P.scope():
      CHM = 1024
      stage = P.sb("stage", [128, 2048], F32)
      sbf = P.sb("sbf", [128, 2048], BF16)
      xnbf = [P.sb(f"xnbf{k}", [128, CHM], BF16) for k in range(8)]
      gw = [P.sb(f"gw{k}", [128, 8], F32) for k in range(8)]
      identf = P.sb("idf", [128, 128], F32)
      gT = P.sb("gT", [8, CHM], F32)
      gN = P.sb("gN", [128, 8], F32)
      e_t = P.sb("e_t", [128, 8], F32)
      negmax = P.sb("negmax", [128, 1], F32)
      ssum = P.sb("ssum", [128, 1], F32)
      tvf = P.sb("tvf", [128, 1], F32)
      mN = P.sb("mN", [128, 8], F32)
      mT = P.sb("mT", [1, 8 * CHM], BF16)
      tvrow = P.sb("tvrow", [1, CHM], F32)
      m_bf = P.sb("m_bf", [128, CHM], BF16)
      xe = [P.sb(f"xe{k}", [128, 512], BF16) for k in range(8)]
      w1sb = [P.sb(f"w1sb{k}", [128, 2048], BF16) for k in range(8)]
      w2sb = [P.sb(f"w2sb{k}", [128, 1024], BF16) for k in range(16)]
      he = [P.sb(f"he{k}", [128, 512], BF16) for k in range(16)]
      acc = [P.sb(f"acc{k}", [128, CHM], BF16) for k in range(8)]
      tmp_bf = P.sb("tmp_bf", [128, 512], BF16)
      repf = P.sb("repf", [128, 512], F32)
      outb = P.sb("outb", [128, 512], F32)
      ones_row = P.sb("onesrow", [1, 128], F32)
      ones_row_bf = P.sb("onesrowbf", [1, 128], BF16)
      pa = P.ps("pa", [128, 512])
      pb = P.ps("pb", [128, 512])
      pt = P.ps("pt", [128, 128])

      P.memset(ones_row[:, :], 1.0)
      P.memset(ones_row_bf[:, :], 1.0)
      P.dma("sync", identf[:, :], a["identf"][:, :])
      for k in range(8):
          P.dma("sync", stage[:, 0:8], a["gate_wT"][k * 128:(k + 1) * 128, :])
          P.step()
          P.copy(gw[k][:, :], stage[:, 0:8], eng="vector")
          P.step()
      for e in range(8):
          for kt in range(8):
              P.dma("sync", stage[:, :], a["ew1"][e, kt * 128:(kt + 1) * 128, :])
              P.step()
              P.copy(sbf[:, :], stage[:, :])
              P.step()
              P.dma("sync", a["w1s"][e, kt * 128:(kt + 1) * 128, :], sbf[:, :])
              P.step()
          for kt in range(16):
              P.dma("sync", stage[:, 0:1024], a["ew2"][e, kt * 128:(kt + 1) * 128, :])
              P.step()
              P.copy(sbf[:, 0:1024], stage[:, 0:1024])
              P.step()
              P.dma("sync", a["w2s"][e, kt * 128:(kt + 1) * 128, :], sbf[:, 0:1024])
              P.step()
      for c in range(T // CHM):
          sl = slice(c * CHM, c * CHM + CHM)
          for k in range(8):
              P.dma("sync", stage[:, 0:CHM], a["xnT"][k * 128:(k + 1) * 128, sl])
              P.step()
              P.copy(xnbf[k][:, :], stage[:, 0:CHM])
              P.memset(acc[k][:, :], 0.0, eng="gpsimd")
              P.step()
          for sub in range(CHM // 512):
              for k in range(8):
                  P.dma("sync", outb[:, :],
                        a["xnT"][k * 128:(k + 1) * 128,
                                 c * CHM + sub * 512:c * CHM + (sub + 1) * 512])
                  P.step()
                  P.mm(pa[0:8, :], gw[k][:, :], outb[:, :],
                       start=(k == 0), stop=(k == 7))
                  P.step()
              P.copy(gT[:, sub * 512:(sub + 1) * 512], pa[0:8, :], eng="vector")
              P.step()
          for j in range(CHM // 128):
              jsl = slice(j * 128, (j + 1) * 128)
              P.transpose(pt[:, 0:8], gT[:, jsl], identf[0:8, 0:8])
              P.step()
              P.copy(gN[:, :], pt[:, 0:8], eng="vector")
              P.step()
              P.red(negmax[:, :], gN[:, :], OP.max, negate=True)
              P.step()
              P.act(e_t[:, :], gN[:, :], AF.Exp, bias=negmax[:, 0:1])
              P.step()
              P.red(ssum[:, :], e_t[:, :], OP.add)
              P.step()
              P.ts2(tvf[:, :], ssum[:, :], 1e-8, 1.0, OP.mult, OP.add)
              P.step()
              P.act(tvf[:, :], tvf[:, :], AF.Ln)
              P.step()
              P.act(tvf[:, :], tvf[:, :], AF.Exp, scale=-1.0)
              P.step()
              P.ts2(mN[:, :], gN[:, :], -1.0, negmax[:, 0:1], OP.mult, OP.is_le)
              P.step()
              for e in range(8):
                  dst = pa if e < 4 else pb
                  P.transpose(dst[0:1, (e % 4) * 128:(e % 4 + 1) * 128], mN[:, e:e + 1], identf[:, :])
              P.step()
              for e in range(8):
                  src = pa if e < 4 else pb
                  P.copy(mT[0:1, e * CHM + j * 128: e * CHM + (j + 1) * 128],
                         src[0:1, (e % 4) * 128:(e % 4 + 1) * 128], eng="vector")
              P.step()
              P.transpose(pt[0:1, 0:128], tvf[:, :], identf[:, :])
              P.step()
              P.copy(tvrow[0:1, jsl], pt[0:1, 0:128], eng="vector")
              P.step()
          for e in range(8):
              for k in range(8):
                  P.dma("sync", w1sb[k][:, :], a["w1s"][e, k * 128:(k + 1) * 128, :])
              for k in range(16):
                  P.dma("gpsimd", w2sb[k][:, :], a["w2s"][e, k * 128:(k + 1) * 128, :])
              for sub in range(CHM // 512):
                  P.mm(pa[:, :], ones_row_bf[0:1, :],
                       mT[0:1, e * CHM + sub * 512: e * CHM + (sub + 1) * 512],
                       start=True, stop=True)
                  P.step()
                  P.copy(m_bf[:, sub * 512:(sub + 1) * 512], pa[:, :])
                  P.step()
              for nsub in range(CHM // 512):
                  nsl = slice(nsub * 512, (nsub + 1) * 512)
                  for k in range(8):
                      P.tt(xe[k][:, :], xnbf[k][:, nsl], m_bf[:, nsl], OP.mult,
                           eng=("vector" if k % 2 == 0 else "gpsimd"))
                  P.step()
                  for ht in range(16):
                      for k in range(8):
                          P.mm(pa[:, :], w1sb[k][:, ht * 128:(ht + 1) * 128], xe[k][:, :],
                               start=(k == 0), stop=(k == 7))
                      P.step()
                      P.act(he[ht][:, :], pa[:, :], AF.Gelu)
                      P.step()
                  for ot in range(8):
                      for h in range(16):
                          P.mm(pb[:, :], w2sb[h][:, ot * 128:(ot + 1) * 128], he[h][:, :],
                               start=(h == 0), stop=(h == 15))
                      P.step()
                      P.copy(tmp_bf[:, :], pb[:, :])
                      P.step()
                      P.tt(acc[ot][:, nsl], acc[ot][:, nsl], tmp_bf[:, :], OP.add)
                      P.step()
          for sub in range(CHM // 512):
              nsl = slice(sub * 512, (sub + 1) * 512)
              P.mm(pa[:, :], ones_row[0:1, :], tvrow[0:1, nsl], start=True, stop=True)
              P.step()
              P.copy(repf[:, :], pa[:, :], eng="vector")
              P.step()
              for ot in range(8):
                  P.tt(outb[:, :], acc[ot][:, nsl], repf[:, :], OP.mult)
                  P.step()
                  P.dma("sync", a["moeT"][ot * 128:(ot + 1) * 128,
                                          c * CHM + sub * 512: c * CHM + (sub + 1) * 512],
                        outb[:, :])
                  P.step()


def build_l0moe():
    nc, a = _mk_nc(
        "l0moe",
        ins={"xnT": ((D, T), F32), "gate_wT": ((D, 8), F32),
             "ew1": ((8, D, 2048), F32), "ew2": ((8, 2048, D), F32),
             "identf": ((128, 128), F32)},
        outs={"moeT": ((D, T), F32)},
    )
    a["w1s"] = nc.dram_tensor("w1s", [8, D, 2048], BF16, kind="Internal").ap()
    a["w2s"] = nc.dram_tensor("w2s", [8, 2048, D], BF16, kind="Internal").ap()
    P = Prog(nc)
    stage_l0moe(P, a)
    P.finish()
    return nc


def stage_glue0(P, a):
  with P.scope():
      CHL = 512
      agbf = P.sb("agbf", [128, 512], BF16) if "agin" in a else None
      lng_t = [P.sb(f"lng{k}", [128, 1], F32) for k in range(8)]
      lnb_t = [P.sb(f"lnb{k}", [128, 1], F32) for k in range(8)]
      ones_col = P.sb("onescol", [128, 1], F32)
      ones_row = P.sb("onesrow", [1, 128], F32)
      m_t = [P.sb(f"m{k}", [128, CHL], F32) for k in range(8)]
      x_t = [P.sb(f"x{k}", [128, CHL], F32) for k in range(8)]
      s_t = [P.sb(f"s{k}", [128, CHL], F32) for k in range(8)]
      sq = P.sb("sq", [128, CHL], F32)
      rep_a = P.sb("repa", [128, CHL], F32)
      rep_b = P.sb("repb", [128, CHL], F32)
      rstd_row = P.sb("rstd", [1, CHL], F32)
      nm_row = P.sb("nm", [1, CHL], F32)
      pa = P.ps("pa", [128, 512])
      pr = P.ps("pr", [128, 512])

      P.memset(ones_col[:, :], 1.0)
      P.memset(ones_row[:, :], 1.0)
      load_pvec(P, a["lng"], lng_t)
      load_pvec(P, a["lnb"], lnb_t)
      for c in range(T // CHL):
          sl = slice(c * CHL, c * CHL + CHL)
          for k in range(8):
              P.dma("sync", m_t[k][:, :], a["moeT"][k * 128:(k + 1) * 128, sl])
              P.dma("gpsimd", x_t[k][:, :], a["xT"][k * 128:(k + 1) * 128, sl])
              P.dma("scalar", s_t[k][:, :], a["sT"][k * 128:(k + 1) * 128, sl])
          P.step()
          ln_rows(P, m_t, CHL, sq, pa, ones_col, ones_row, rstd_row, nm_row,
                  rep_a, rep_b, 1e-5, pr)
          for k in range(8):
              P.tt(m_t[k][:, :], m_t[k][:, :], rep_a[:, :], OP.mult)
              P.step()
              P.tt(m_t[k][:, :], m_t[k][:, :], rep_b[:, :], OP.add)
              P.step()
              P.ts2(m_t[k][:, :], m_t[k][:, :], lng_t[k][:, 0:1], lnb_t[k][:, 0:1],
                    OP.mult, OP.add)
              P.step()
              P.act(m_t[k][:, :], m_t[k][:, :], AF.Silu)
              P.step()
              P.tt(m_t[k][:, :], m_t[k][:, :], s_t[k][:, :], OP.mult)
              P.step()
              P.tt(x_t[k][:, :], x_t[k][:, :], m_t[k][:, :], OP.add)
              P.step()
              P.dma("sync", a["x1T"][k * 128:(k + 1) * 128, sl], x_t[k][:, :])
          P.step()
          ln_rows(P, x_t, CHL, sq, pa, ones_col, ones_row, rstd_row, nm_row,
                  rep_a, rep_b, 1e-5, pr)
          for k in range(8):
              P.tt(m_t[k][:, :], x_t[k][:, :], rep_a[:, :], OP.mult)
              P.step()
              P.tt(m_t[k][:, :], m_t[k][:, :], rep_b[:, :], OP.add)
              P.step()
              P.dma("sync", a["xn1T"][k * 128:(k + 1) * 128, sl], m_t[k][:, :])
              if "agin" in a:
                  P.copy(agbf[:, :], m_t[k][:, :])
                  P.step()
                  P.dma("gpsimd", a["agin"][k * 128:(k + 1) * 128, sl], agbf[:, :])
              P.step()


def build_glue0():
    nc, a = _mk_nc(
        "glue0",
        ins={"moeT": ((D, T), F32), "xT": ((D, T), F32), "sT": ((D, T), F32),
             "lng": ((D, 1), F32), "lnb": ((D, 1), F32)},
        outs={"x1T": ((D, T), F32), "xn1T": ((D, T), F32)},
    )
    P = Prog(nc)
    stage_glue0(P, a)
    P.finish()
    return nc


def stage_l1attn(P, a):
  fused = "xnp2" in a
  with P.scope():
      CHA = 512
      win = [P.sb(f"win{k}", [128, 3 * D], BF16) for k in range(8)]
      wout = [P.sb(f"wout{k}", [128, D], BF16) for k in range(8)]
      stage = P.sb("stage", [128, 3 * D], F32)
      xnbf = [P.sb(f"xnbf{k}", [128, CHA], BF16) for k in range(8)]
      qkvbf = [P.sb(f"qkv{m}", [128, CHA], BF16) for m in range(24)]
      oT = [P.sb(f"oT{k}", [128, CHA], BF16) for k in range(8)]
      maskrep = P.sb("maskrep", [128, 512], F32)
      identbf = P.sb("identbf", [128, 128], BF16)
      ssb = P.sb("ssb", [128, 512], F32)
      Eb = P.sb("Eb", [128, 512], F32)
      nm = P.sb("nm", [128, 4], F32)
      sm = P.sb("sm", [128, 4], F32)
      Pbf = P.sb("Pbf", [128, 512], BF16)
      Ptb = P.sb("Ptb", [128, 512], BF16)
      vnb = P.sb("vnb", [128, 256], BF16)
      sout = P.sb("sout", [128, 512], F32)
      sout_bf = P.sb("sout_bf", [128, 512], BF16)
      pa = P.ps("pa", [128, 512])
      ps_s = P.ps("ps_s", [128, 512])
      ps_t = P.ps("ps_t", [128, 512], BF16)
      ps_vn = P.ps("ps_vn", [128, 256], BF16)
      ps_o = P.ps("ps_o", [64, 512])

      P.dma("sync", maskrep[:, :], a["maskrep"][:, :])
      P.dma("sync", identbf[:, :], a["identbf"][:, :])
      load_wbf(P, a["WinT"], 8, 3 * D, stage, win)
      load_wbf(P, a["WoutT"], 8, D, stage, wout)

      for c in range(T // CHA):
          sl = slice(c * CHA, c * CHA + CHA)
          if fused:
              pos0 = c * 64
              for k in range(8):
                  dst4 = xnbf[k][:, :].rearrange("p (g b q) -> p g b q", b=8, q=16)
                  for b in range(8):
                      src3 = a["xnp2"][b, k * 128:(k + 1) * 128, pos0:pos0 + 64]
                      src3 = src3.rearrange("p (g q) -> p g q", q=16)
                      P.dma("sync" if b % 2 == 0 else "gpsimd", dst4[:, :, b, :], src3)
              P.step()
          else:
              for k in range(8):
                  P.dma("sync", stage[:, 0:CHA], a["xnp"][k * 128:(k + 1) * 128, sl])
                  P.step()
                  P.copy(xnbf[k][:, :], stage[:, 0:CHA])
                  P.step()
          for mt in range(24):
              for k in range(8):
                  P.mm(pa[:, :], win[k][:, mt * 128:(mt + 1) * 128], xnbf[k][:, :],
                       start=(k == 0), stop=(k == 7))
              P.step()
              if mt < 8:
                  P.act(qkvbf[mt][:, :], pa[:, :], AF.Copy, scale=0.125)
              else:
                  P.copy(qkvbf[mt][:, :], pa[:, :])
              P.step()
          for blk in range(4):
              bs = slice(blk * 128, (blk + 1) * 128)
              for w4 in range(4):
                  hs = [[0, 2, 4, 6], [8, 10, 12, 14], [1, 3, 5, 7], [9, 11, 13, 15]][w4]
                  for i, h in enumerate(hs):
                      ro = (h % 2) * 64
                      P.mm(ps_s[:, i * 128:(i + 1) * 128],
                           qkvbf[h // 2][ro:ro + 64, bs],
                           qkvbf[8 + h // 2][ro:ro + 64, bs], start=True, stop=True)
                  P.step()
                  P.tt(ssb[:, :], ps_s[:, :], maskrep[:, :], OP.add)
                  P.step()
                  for i in range(4):
                      P.red(nm[:, i:i + 1], ssb[:, i * 128:(i + 1) * 128], OP.max, negate=True)
                  P.step()
                  for i in range(4):
                      P.act(Eb[:, i * 128:(i + 1) * 128], ssb[:, i * 128:(i + 1) * 128],
                            AF.Exp, bias=nm[:, i:i + 1])
                  P.step()
                  for i in range(4):
                      P.red(sm[:, i:i + 1], Eb[:, i * 128:(i + 1) * 128], OP.add)
                  P.step()
                  P.act(sm[:, :], sm[:, :], AF.Ln)
                  P.step()
                  P.act(sm[:, :], sm[:, :], AF.Exp, scale=-1.0)
                  P.step()
                  for i in range(4):
                      P.ts(Pbf[:, i * 128:(i + 1) * 128], Eb[:, i * 128:(i + 1) * 128],
                           sm[:, i:i + 1], OP.mult)
                  P.step()
                  for i, h in enumerate(hs):
                      P.transpose(ps_t[:, i * 128:(i + 1) * 128], Pbf[:, i * 128:(i + 1) * 128],
                                  identbf[:, :])
                  P.step()
                  for i, h in enumerate(hs):
                      ro = (h % 2) * 64
                      P.transpose(ps_vn[:, i * 64:(i + 1) * 64],
                                  qkvbf[16 + h // 2][ro:ro + 64, bs],
                                  identbf[ro:ro + 64, ro:ro + 64])
                  P.step()
                  P.copy(Ptb[:, :], ps_t[:, :])
                  P.copy(vnb[:, :], ps_vn[:, :], eng="vector")
                  P.step()
                  for i in range(4):
                      P.mm(ps_o[0:64, i * 128:(i + 1) * 128], vnb[:, i * 64:(i + 1) * 64],
                           Ptb[:, i * 128:(i + 1) * 128], start=True, stop=True)
                  P.step()
                  for i, h in enumerate(hs):
                      ro = (h % 2) * 64
                      P.copy(oT[h // 2][ro:ro + 64, bs], ps_o[0:64, i * 128:(i + 1) * 128])
                  P.step()
          for ot in range(8):
              for k in range(8):
                  P.mm(pa[:, :], wout[k][:, ot * 128:(ot + 1) * 128], oT[k][:, :],
                       start=(k == 0), stop=(k == 7))
              P.step()
              if fused:
                  P.copy(sout_bf[:, :], pa[:, :])
                  P.step()
                  pos0 = c * 64
                  src4 = sout_bf[:, :].rearrange("p (g b q) -> p g b q", b=8, q=16)
                  for b in range(8):
                      dst3 = a["s1loc"][b * D + ot * 128: b * D + (ot + 1) * 128,
                                        pos0:pos0 + 64]
                      dst3 = dst3.rearrange("p (g q) -> p g q", q=16)
                      P.dma("sync" if b % 2 == 0 else "gpsimd", dst3, src4[:, :, b, :])
                  P.step()
              else:
                  P.copy(sout[:, :], pa[:, :], eng="vector")
                  P.step()
                  P.dma("sync", a["s1p"][ot * 128:(ot + 1) * 128, sl], sout[:, :])
                  P.step()


def build_l1attn():
    nc, a = _mk_nc(
        "l1attn",
        ins={"xnp": ((D, T), F32), "WinT": ((D, 3 * D), F32), "WoutT": ((D, D), F32),
             "maskrep": ((128, 512), F32), "identbf": ((128, 128), BF16)},
        outs={"s1p": ((D, T), F32)},
    )
    P = Prog(nc)
    stage_l1attn(P, a)
    P.finish()
    return nc


def stage_l1fa(P, a):
  with P.scope():
      CHF = 512
      wqkv = [P.sb(f"wqkv{k}", [128, 1536], BF16) for k in range(8)]
      wo = [P.sb(f"wo{k}", [128, D], BF16) for k in range(4)]
      stage = P.sb("stage", [128, 1536], F32)
      permT = P.sb("permT", [128, 128], BF16)
      selq = P.sb("selq", [2, 128], F32)
      selk = P.sb("selk", [2, 128], F32)
      blko = P.sb("blko", [128, 2], F32)
      mp = [P.sb(f"mp{i}", [128, 512], F32) for i in range(4)]
      mc = [P.sb(f"mc{i}", [128, 512], F32) for i in range(4)]
      tabs = {n: P.sb(f"tab{n}", [128, CHF], F32) for n in ("qc", "qs", "kc", "ks")}
      x_t = [P.sb(f"x{k}", [128, CHF], F32) for k in range(8)]
      xs_t = [P.sb(f"xs{k}", [128, CHF], F32) for k in range(4)]
      h_bf = [P.sb(f"hbf{k}", [128, CHF], BF16) for k in range(8)]
      qT = [P.sb(f"qT{k}", [128, CHF], BF16) for k in range(4)]
      kT = [[P.sb(f"kT{p}_{k}", [128, CHF], BF16) for k in range(4)] for p in range(2)]
      vN = [[P.sb(f"vN{p}_{k}", [128, CHF], BF16) for k in range(4)] for p in range(2)]
      est = [P.sb(f"est{i}", [128, CHF], BF16) for i in range(8)]
      oT = [P.sb(f"oTf{k}", [128, CHF], BF16) for k in range(4)]
      sq = P.sb("sq", [128, CHF], F32)
      t2 = P.sb("t2", [128, CHF], F32)
      qn = P.sb("qn", [128, CHF], F32)
      qn_bf = P.sb("qn_bf", [128, CHF], BF16)
      ssrow = P.sb("ssrow", [2, 512], F32)
      invrow = P.sb("invrow", [1, 512], F32)
      repf = P.sb("repf", [128, 512], F32)
      repbf = P.sb("repbf", [128, 512], BF16)
      xmout = P.sb("xmout", [128, CHF], F32)
      ones_col_bf = P.sb("onescolbf", [128, 1], BF16)
      ones_row = P.sb("onesrow", [1, 128], F32)
      pm = P.ps("pm", [128, 512])
      pst = P.ps("pst", [128, 512])
      prep = P.ps("prep", [128, 512])
      prot = P.ps("prot", [128, 512])
      po = P.ps("po", [64, 512])

      P.memset(ones_col_bf[:, :], 1.0)
      P.memset(ones_row[:, :], 1.0)
      P.memset(blko[:, :], 0.0)
      P.memset(selq[:, :], 0.0)
      P.memset(selk[:, :], 0.0)
      P.step()
      P.memset(blko[0:64, 0:1], 1.0)
      P.step()
      P.memset(blko[64:128, 1:2], 1.0)
      P.step()
      P.dma("sync", selq[0:1, 0:64], a["qsv"][0:1, :])
      P.dma("sync", selk[0:1, 0:64], a["ksv"][0:1, :])
      P.step()
      P.dma("sync", selq[1:2, 64:128], a["qsv"][0:1, :])
      P.dma("sync", selk[1:2, 64:128], a["ksv"][0:1, :])
      P.dma("sync", permT[:, :], a["permT"][:, :])
      for i in range(4):
          P.dma("gpsimd", mp[i][:, :], a["mprev"][i * 128:(i + 1) * 128, :])
          P.dma("gpsimd", mc[i][:, :], a["mcur"][i * 128:(i + 1) * 128, :])
      P.step()
      load_wbf(P, a["wqkv"], 8, 1536, stage, wqkv)
      load_wbf(P, a["wo"], 4, D, stage, wo)

      for c in range(T // CHF):
          c0 = c * CHF
          sl = slice(c0, c0 + CHF)
          par = c % 2
          for k in range(8):
              P.dma("sync", x_t[k][:, :], a["xinT"][k * 128:(k + 1) * 128, sl])
          for n, t in tabs.items():
              P.dma("scalar", t[:, :], a["t" + n][:, sl])
          if c == 0:
              for k in range(4):
                  P.memset(xs_t[k][:, 0:1], 0.0, eng="gpsimd")
                  P.dma("gpsimd", xs_t[k][:, 1:CHF],
                        a["xinT"][(k + 4) * 128:(k + 5) * 128, 0:CHF - 1])
          else:
              for k in range(4):
                  P.dma("gpsimd", xs_t[k][:, :],
                        a["xinT"][(k + 4) * 128:(k + 5) * 128, c0 - 1:c0 + CHF - 1])
          P.step()
          for k in range(4):
              P.copy(h_bf[k][:, :], x_t[k][:, :])
              P.copy(h_bf[k + 4][:, :], xs_t[k][:, :], eng="vector")
          P.step()
          for mt in range(8):
              isq = mt < 4
              for k in range(8):
                  P.mm(pm[:, :], wqkv[k][:, mt * 128:(mt + 1) * 128], h_bf[k][:, :],
                       start=(k == 0), stop=(k == 7))
              P.step()
              P.act(sq[:, :], pm[:, :], AF.Square)
              P.step()
              P.mm(prep[0:2, :], blko[:, :], sq[:, :], start=True, stop=True)
              P.step()
              P.copy(ssrow[:, :], prep[0:2, :], eng="vector")
              P.step()
              P.act(ssrow[:, :], ssrow[:, :], AF.Ln)
              P.step()
              P.act(ssrow[:, :], ssrow[:, :], AF.Exp, scale=-0.5)
              P.step()
              P.ts(ssrow[:, :], ssrow[:, :], 1e12, OP.min)
              P.step()
              P.mm(prep[:, :], selq[:, :] if isq else selk[:, :], ssrow[:, :],
                   start=True, stop=True)
              P.step()
              P.copy(repf[:, :], prep[:, :], eng="vector")
              P.step()
              P.tt(qn[:, :], pm[:, :], repf[:, :], OP.mult)
              P.step()
              P.copy(qn_bf[:, :], qn[:, :])
              P.step()
              P.mm(prot[:, :], permT[:, :], qn_bf[:, :], start=True, stop=True)
              P.step()
              P.tt(t2[:, :], qn[:, :], tabs["qc" if isq else "kc"][:, :], OP.mult)
              P.step()
              P.tt(sq[:, :], prot[:, :], tabs["qs" if isq else "ks"][:, :], OP.mult)
              P.step()
              dst = qT[mt] if isq else kT[par][mt - 4]
              P.tt(dst[:, :], t2[:, :], sq[:, :], OP.add)
              P.step()
          for tt_ in range(4):
              for k in range(8):
                  P.mm(pm[:, :], h_bf[k][:, tt_ * 128:(tt_ + 1) * 128],
                       wqkv[k][:, 1024:1536], start=(k == 0), stop=(k == 7))
              P.step()
              P.copy(vN[par][tt_][:, :], pm[:, :])
              P.step()
          jts = ([] if c == 0 else [(1 - par, jb, mp[jb]) for jb in range(4)]) + \
                [(par, jb, mc[jb]) for jb in range(4)]
          for h in range(8):
              ro = (h % 2) * 64
              for ji, (p_, jb, msk) in enumerate(jts):
                  P.mm(pst[:, :], kT[p_][h // 2][ro:ro + 64, jb * 128:(jb + 1) * 128],
                       qT[h // 2][ro:ro + 64, :], start=True, stop=True)
                  P.step()
                  P.tt(pst[:, :], pst[:, :], msk[:, :], OP.add)
                  P.step()
                  P.act(est[ji][:, :], pst[:, :], AF.Exp)
                  P.step()
              for ji in range(len(jts)):
                  P.mm(prep[0:1, :], ones_col_bf[:, :], est[ji][:, :],
                       start=(ji == 0), stop=(ji == len(jts) - 1))
              P.step()
              P.copy(invrow[:, :], prep[0:1, :], eng="vector")
              P.step()
              P.act(invrow[:, :], invrow[:, :], AF.Ln)
              P.step()
              P.act(invrow[:, :], invrow[:, :], AF.Exp, scale=-1.0)
              P.step()
              P.mm(prep[:, :], ones_row[0:1, :], invrow[0:1, :], start=True, stop=True)
              P.step()
              P.copy(repbf[:, :], prep[:, :])
              P.step()
              for ji in range(len(jts)):
                  P.tt(est[ji][:, :], est[ji][:, :], repbf[:, :], OP.mult,
                       eng=("vector" if ji % 2 == 0 else "gpsimd"))
              P.step()
              for ji, (p_, jb, msk) in enumerate(jts):
                  P.mm(po[0:64, :], vN[p_][jb][:, h * 64:(h + 1) * 64], est[ji][:, :],
                       start=(ji == 0), stop=(ji == len(jts) - 1))
              P.step()
              P.copy(oT[h // 2][ro:ro + 64, :], po[0:64, :])
              P.step()
          for ot in range(8):
              for k in range(4):
                  P.mm(pm[:, :], wo[k][:, ot * 128:(ot + 1) * 128], oT[k][:, :],
                       start=(k == 0), stop=(k == 3))
              P.step()
              P.tt(xmout[:, :], pm[:, :], x_t[ot][:, :], OP.add)
              P.step()
              P.dma("sync", a["xmidT"][ot * 128:(ot + 1) * 128, sl], xmout[:, :])
              P.step()


def build_l1fa():
    nc, a = _mk_nc(
        "l1fa",
        ins={"xinT": ((D, T), F32), "wqkv": ((D, 1536), F32), "wo": ((512, D), F32),
             "qsv": ((1, 64), F32), "ksv": ((1, 64), F32),
             "tqc": ((128, T), F32), "tqs": ((128, T), F32),
             "tkc": ((128, T), F32), "tks": ((128, T), F32),
             "mprev": ((512, 512), F32), "mcur": ((512, 512), F32),
             "permT": ((128, 128), BF16)},
        outs={"xmidT": ((D, T), F32)},
    )
    P = Prog(nc)
    stage_l1fa(P, a)
    P.finish()
    return nc


def stage_l1fb1(P, a):
  with P.scope():
      CHF = 512
      NMT = 22
      w1a = [P.sb(f"w1a{k}", [128, 2730], BF16) for k in range(8)]
      w1g = [P.sb(f"w1g{k}", [128, 2730], BF16) for k in range(8)]
      stage = P.sb("stage", [128, 2730], F32)
      lng_t = [P.sb(f"lng{k}", [128, 1], F32) for k in range(8)]
      lnb_t = [P.sb(f"lnb{k}", [128, 1], F32) for k in range(8)]
      b1a_t = [P.sb(f"b1a{k}", [128, 1], F32) for k in range(NMT)]
      b1g_t = [P.sb(f"b1g{k}", [128, 1], F32) for k in range(NMT)]
      ones_col = P.sb("onescol", [128, 1], F32)
      ones_row = P.sb("onesrow", [1, 128], F32)
      xh = [P.sb(f"xh{k}", [128, CHF], F32) for k in range(8)]
      hn_bf = [P.sb(f"hnbf{k}", [128, CHF], BF16) for k in range(8)]
      sq = P.sb("sq", [128, CHF], F32)
      t2 = P.sb("t2", [128, CHF], F32)
      hebuf = P.sb("hebuf", [128, CHF], BF16)
      rep_a = P.sb("repa", [128, CHF], F32)
      rep_b = P.sb("repb", [128, CHF], F32)
      rstd_row = P.sb("rstd", [1, CHF], F32)
      nm_row = P.sb("nm", [1, CHF], F32)
      pa = P.ps("pa", [128, 512])
      pg = P.ps("pg", [128, 512])
      pr = P.ps("pr", [128, 512])

      P.memset(ones_col[:, :], 1.0)
      P.memset(ones_row[:, :], 1.0)
      load_pvec(P, a["lng"], lng_t)
      load_pvec(P, a["lnb"], lnb_t)
      for mt in range(NMT):
          m = 128 if mt < NMT - 1 else 42
          P.dma("sync", b1a_t[mt][0:m, :], a["b1a"][mt * 128:mt * 128 + m, :])
          P.dma("gpsimd", b1g_t[mt][0:m, :], a["b1g"][mt * 128:mt * 128 + m, :])
      P.step()
      load_wbf(P, a["w1a"], 8, 2730, stage, w1a)
      load_wbf(P, a["w1g"], 8, 2730, stage, w1g)

      for c in range(T // CHF):
          c0 = c * CHF
          sl = slice(c0, c0 + CHF)
          for k in range(4):
              P.dma("sync", xh[k][:, :], a["xmidT"][k * 128:(k + 1) * 128, sl])
          if c == 0:
              for k in range(4):
                  P.memset(xh[k + 4][:, 0:1], 0.0, eng="gpsimd")
                  P.dma("gpsimd", xh[k + 4][:, 1:CHF],
                        a["xmidT"][(k + 4) * 128:(k + 5) * 128, 0:CHF - 1])
          else:
              for k in range(4):
                  P.dma("gpsimd", xh[k + 4][:, :],
                        a["xmidT"][(k + 4) * 128:(k + 5) * 128, c0 - 1:c0 + CHF - 1])
          P.step()
          ln_rows(P, xh, CHF, sq, pa, ones_col, ones_row, rstd_row, nm_row,
                  rep_a, rep_b, 1e-5, pr)
          for k in range(8):
              P.tt(xh[k][:, :], xh[k][:, :], rep_a[:, :], OP.mult)
              P.step()
              P.tt(xh[k][:, :], xh[k][:, :], rep_b[:, :], OP.add)
              P.step()
              P.ts2(xh[k][:, :], xh[k][:, :], lng_t[k][:, 0:1], lnb_t[k][:, 0:1],
                    OP.mult, OP.add)
              P.step()
              P.copy(hn_bf[k][:, :], xh[k][:, :])
              P.step()
          for mt in range(NMT):
              m = 128 if mt < NMT - 1 else 42
              msl = slice(mt * 128, mt * 128 + m)
              for k in range(8):
                  P.mm(pa[0:m, :], w1a[k][:, msl], hn_bf[k][:, :],
                       start=(k == 0), stop=(k == 7))
                  P.mm(pg[0:m, :], w1g[k][:, msl], hn_bf[k][:, :],
                       start=(k == 0), stop=(k == 7))
              P.step()
              P.act(sq[0:m, :], pg[0:m, :], AF.Gelu, bias=b1g_t[mt][0:m, 0:1])
              P.step()
              P.ts(t2[0:m, :], pa[0:m, :], b1a_t[mt][0:m, 0:1], OP.add)
              P.step()
              P.tt(hebuf[0:m, :], t2[0:m, :], sq[0:m, :], OP.mult)
              P.step()
              P.dma("sync", a["heT"][mt * 128:mt * 128 + m, sl], hebuf[0:m, :])
              P.step()


def build_l1fb1():
    nc, a = _mk_nc(
        "l1fb1",
        ins={"xmidT": ((D, T), F32), "lng": ((D, 1), F32), "lnb": ((D, 1), F32),
             "w1a": ((D, 2730), F32), "w1g": ((D, 2730), F32),
             "b1a": ((2730, 1), F32), "b1g": ((2730, 1), F32)},
        outs={"heT": ((2816, T), BF16)},
    )
    P = Prog(nc)
    stage_l1fb1(P, a)
    P.finish()
    return nc


def stage_l1fb2(P, a, final):
  s1bf16 = "s1bfT" in a
  with P.scope():
      CHF = 512
      s1b = P.sb("s1b", [128, 512], BF16) if s1bf16 else None
      NMT = 22
      w2 = [P.sb(f"w2{k}", [128, D], BF16) for k in range(NMT)]
      stage = P.sb("stage", [128, D], F32)
      b2_t = [P.sb(f"b2{k}", [128, 1], F32) for k in range(8)]
      he_t = [P.sb(f"he{k}", [128, CHF], BF16) for k in range(NMT)]
      xm = P.sb("xm", [128, CHF], F32)
      x1 = P.sb("x1", [128, CHF], F32)
      s1 = P.sb("s1", [128, CHF], F32)
      t2 = P.sb("t2", [128, CHF], F32)
      yb = P.sb("yb", [128, CHF], F32)
      pa = P.ps("pa", [128, 512])

      load_pvec(P, a["b2"], b2_t)
      for kt in range(NMT):
          P.dma("sync", stage[:, :], a["w2"][kt * 128:(kt + 1) * 128, :])
          P.step()
          P.copy(w2[kt][:, :], stage[:, :])
          P.step()
      for c in range(T // CHF):
          sl = slice(c * CHF, c * CHF + CHF)
          for kt in range(NMT):
              P.dma("sync", he_t[kt][:, :], a["heT"][kt * 128:(kt + 1) * 128, sl])
          P.step()
          for ot in range(8):
              for kt in range(NMT):
                  P.mm(pa[:, :], w2[kt][:, ot * 128:(ot + 1) * 128], he_t[kt][:, :],
                       start=(kt == 0), stop=(kt == NMT - 1))
              P.dma("gpsimd", xm[:, :], a["xmidT"][ot * 128:(ot + 1) * 128, sl])
              if final:
                  P.dma("scalar", x1[:, :], a["x1T"][ot * 128:(ot + 1) * 128, sl])
                  if s1bf16:
                      P.dma("sync", s1b[:, :], a["s1bfT"][ot * 128:(ot + 1) * 128, sl])
                  else:
                      P.dma("sync", s1[:, :], a["s1T"][ot * 128:(ot + 1) * 128, sl])
              P.step()
              if final and s1bf16:
                  P.copy(s1[:, :], s1b[:, :])
                  P.step()
              P.ts(yb[:, :], pa[:, :], b2_t[ot][:, 0:1], OP.add)
              P.step()
              P.tt(yb[:, :], yb[:, :], xm[:, :], OP.add)
              P.step()
              if final:
                  P.act(yb[:, :], yb[:, :], AF.Silu)
                  P.step()
                  P.tt(yb[:, :], yb[:, :], s1[:, :], OP.mult)
                  P.step()
                  P.tt(yb[:, :], yb[:, :], x1[:, :], OP.add)
                  P.step()
              P.dma("sync", a["yT"][ot * 128:(ot + 1) * 128, sl], yb[:, :])
              P.step()


def build_l1fb2(final):
    ins = {"heT": ((2816, T), BF16), "xmidT": ((D, T), F32),
           "w2": ((2816, D), F32), "b2": ((D, 1), F32)}
    if final:
        ins.update({"x1T": ((D, T), F32), "s1T": ((D, T), F32)})
    nc, a = _mk_nc("l1fb2", ins=ins, outs={"yT": ((D, T), F32)})
    P = Prog(nc)
    stage_l1fb2(P, a, final)
    P.finish()
    return nc


def build_fused():
    ins = {
        "xT": ((D, T), F32), "WzT": ((D, D), F32), "WzaT": ((D, D), F32),
        "bz": ((D, 1), F32), "bza": ((D, 1), F32),
        "lc_re": ((D, 1), F32), "lc_im": ((D, 1), F32),
        "lam_a": ((D, 1), F32), "lam_b": ((D, 1), F32),
        "tabP": ((D, T), F32), "tabQ": ((D, T), F32),
        "tabU": ((D, T), F32), "tabV": ((D, T), F32),
        "WyT": ((D, D), F32), "WyaT": ((D, D), F32),
        "by": ((D, 1), F32), "bya": ((D, 1), F32), "norm_w": ((D, 1), F32),
        "gate_wT": ((D, 8), F32), "ew1": ((8, D, 2048), F32),
        "ew2": ((8, 2048, D), F32), "identf": ((128, 128), F32),
        "lngm": ((D, 1), F32), "lnbm": ((D, 1), F32),
        "WinT": ((D, 3 * D), F32), "WoutT": ((D, D), F32),
        "maskrep": ((128, 512), F32), "identbf": ((128, 128), BF16),
        "tqc": ((128, T), F32), "tqs": ((128, T), F32),
        "tkc": ((128, T), F32), "tks": ((128, T), F32),
        "mprev": ((512, 512), F32), "mcur": ((512, 512), F32),
        "permT": ((128, 128), BF16),
    }
    for l in range(2):
        ins.update({
            f"wqkv{l}": ((D, 1536), F32), f"wo{l}": ((512, D), F32),
            f"qsv{l}": ((1, 64), F32), f"ksv{l}": ((1, 64), F32),
            f"lng{l}": ((D, 1), F32), f"lnb{l}": ((D, 1), F32),
            f"w1a{l}": ((D, 2730), F32), f"w1g{l}": ((D, 2730), F32),
            f"b1a{l}": ((2730, 1), F32), f"b1g{l}": ((2730, 1), F32),
            f"w2{l}": ((2816, D), F32), f"b2{l}": ((D, 1), F32),
        })
    nc, a = _mk_nc("fused", ins=ins, outs={"outT": ((D, T), F32)})
    it = lambda n, shp, dt: nc.dram_tensor(n, list(shp), dt, kind="Internal").ap()
    xnT = it("xnT", (D, T), F32)
    hT = it("hT", (D, T), F32)
    sT = it("sT", (D, T), F32)
    moeT = it("moeT", (D, T), F32)
    x1T = it("x1T", (D, T), F32)
    xn1T = it("xn1T", (D, T), F32)
    agin = it("agin", (D, T), BF16)
    agout1 = it("agout1", (8 * D, T), BF16)
    xnp2 = it("xnp2", (8, D, 1024), BF16)
    s1loc = it("s1loc", (8 * D, 1024), BF16)
    agout2 = it("agout2", (8 * 8 * D, 1024), BF16)
    s1own = it("s1own", (D, T), BF16)
    xmidT = it("xmidT", (D, T), F32)
    heT = it("heT", (2816, T), BF16)
    y0T = it("y0T", (D, T), F32)
    w1s = it("w1s", (8, D, 2048), BF16)
    w2s = it("w2s", (8, 2048, D), BF16)

    P = Prog(nc)
    ag1_sem = P.st.enter_context(nc.semaphore("ag1"))
    ag2_sem = P.st.enter_context(nc.semaphore("ag2"))

    stage_l0h(P, {"xT": a["xT"], "WzT": a["WzT"], "WzaT": a["WzaT"],
                  "bz": a["bz"], "bza": a["bza"], "lc_re": a["lc_re"],
                  "lc_im": a["lc_im"], "lam_a": a["lam_a"], "lam_b": a["lam_b"],
                  "tabP": a["tabP"], "tabQ": a["tabQ"], "tabU": a["tabU"],
                  "tabV": a["tabV"], "xnT": xnT, "hT": hT})
    stage_l0y(P, {"hT": hT, "xnT": xnT, "WyT": a["WyT"], "WyaT": a["WyaT"],
                  "by": a["by"], "bya": a["bya"], "norm_w": a["norm_w"], "sT": sT})
    stage_l0moe(P, {"xnT": xnT, "gate_wT": a["gate_wT"], "ew1": a["ew1"],
                    "ew2": a["ew2"], "identf": a["identf"], "moeT": moeT,
                    "w1s": w1s, "w2s": w2s})
    stage_glue0(P, {"moeT": moeT, "xT": a["xT"], "sT": sT, "lng": a["lngm"],
                    "lnb": a["lnbm"], "x1T": x1T, "xn1T": xn1T, "agin": agin})
    P._pre("gpsimd")
    inst = P.eng("gpsimd").collective_compute(
        "AllGather", OP.bypass, ins=[agin[:, :]], outs=[agout1[:, :]],
        replica_groups=[list(range(8))])
    inst.then_inc(ag1_sem, 16)
    if "gpsimd" not in P.active:
        P.active["gpsimd"] = None
    P.step()
    stage_l1fa(P, {"xinT": xn1T, "wqkv": a["wqkv0"], "wo": a["wo0"],
                   "qsv": a["qsv0"], "ksv": a["ksv0"], "tqc": a["tqc"],
                   "tqs": a["tqs"], "tkc": a["tkc"], "tks": a["tks"],
                   "mprev": a["mprev"], "mcur": a["mcur"], "permT": a["permT"],
                   "xmidT": xmidT})
    for e in ENGS:
        P.eng(e).wait_ge(ag1_sem, 16)
    P._pre("sync")
    for arm in P.nc.sync.switch_core_id(n=128):
        lg = arm.logical
        if 0 <= lg < 8:
            for b in range(8):
                i2 = P.nc.sync.dma_start(
                    out=xnp2[b, :, :],
                    in_=agout1[b * D:(b + 1) * D, lg * 1024:(lg + 1) * 1024])
                i2.then_inc(P.dma_sem, 16)
    P.step_dmas += 8
    if "sync" not in P.active:
        P.active["sync"] = None
    P.step()
    stage_l1attn(P, {"xnp2": xnp2, "s1loc": s1loc, "WinT": a["WinT"],
                     "WoutT": a["WoutT"], "maskrep": a["maskrep"],
                     "identbf": a["identbf"]})
    P._pre("gpsimd")
    inst = P.eng("gpsimd").collective_compute(
        "AllGather", OP.bypass, ins=[s1loc[:, :]], outs=[agout2[:, :]],
        replica_groups=[list(range(8))])
    inst.then_inc(ag2_sem, 16)
    if "gpsimd" not in P.active:
        P.active["gpsimd"] = None
    P.step()
    stage_l1fb1(P, {"xmidT": xmidT, "lng": a["lng0"], "lnb": a["lnb0"],
                    "w1a": a["w1a0"], "w1g": a["w1g0"], "b1a": a["b1a0"],
                    "b1g": a["b1g0"], "heT": heT})
    stage_l1fb2(P, {"heT": heT, "xmidT": xmidT, "w2": a["w20"], "b2": a["b20"],
                    "yT": y0T}, final=False)
    stage_l1fa(P, {"xinT": y0T, "wqkv": a["wqkv1"], "wo": a["wo1"],
                   "qsv": a["qsv1"], "ksv": a["ksv1"], "tqc": a["tqc"],
                   "tqs": a["tqs"], "tkc": a["tkc"], "tks": a["tks"],
                   "mprev": a["mprev"], "mcur": a["mcur"], "permT": a["permT"],
                   "xmidT": xmidT})
    stage_l1fb1(P, {"xmidT": xmidT, "lng": a["lng1"], "lnb": a["lnb1"],
                    "w1a": a["w1a1"], "w1g": a["w1g1"], "b1a": a["b1a1"],
                    "b1g": a["b1g1"], "heT": heT})
    for e in ENGS:
        P.eng(e).wait_ge(ag2_sem, 16)
    P._pre("sync")
    for arm in P.nc.sync.switch_core_id(n=128):
        lg = arm.logical
        if 0 <= lg < 8:
            for c in range(8):
                i2 = P.nc.sync.dma_start(
                    out=s1own[:, c * 1024:(c + 1) * 1024],
                    in_=agout2[c * 8 * D + lg * D:c * 8 * D + (lg + 1) * D, :])
                i2.then_inc(P.dma_sem, 16)
    P.step_dmas += 8
    if "sync" not in P.active:
        P.active["sync"] = None
    P.step()
    stage_l1fb2(P, {"heT": heT, "xmidT": xmidT, "w2": a["w21"], "b2": a["b21"],
                    "x1T": x1T, "s1bfT": s1own, "yT": a["outT"]}, final=True)
    P.finish()
    return nc


def kernel_fused(inp):
    f32 = np.float32
    asc = np.ascontiguousarray
    C = make_consts()
    x = np.asarray(inp["L0_attn_x"], f32)
    Win = np.asarray(inp["L1_attn_in_proj_weight"], f32)
    perm = np.array([h * 192 + w * 64 + dd for w in range(3) for h in range(16)
                     for dd in range(64)])
    r = np.arange(128)
    maskblk2 = np.where((r[:, None] % 16) == (r[None, :] % 16), 0.0, -1e30).astype(f32)
    w1 = np.asarray(inp["L1_ffn_w1"], f32)
    b1 = np.asarray(inp["L1_ffn_b1"], f32)
    w2in = np.asarray(inp["L1_ffn_w2"], f32)
    base = {
        "WzT": asc(np.asarray(inp["L0_attn_Wz"], f32).T),
        "WzaT": asc(np.asarray(inp["L0_attn_Wza"], f32).T),
        "bz": np.asarray(inp["L0_attn_bz"], f32).reshape(-1, 1),
        "bza": np.asarray(inp["L0_attn_bza"], f32).reshape(-1, 1),
        "lc_re": np.asarray(inp["L0_attn_lc_re"], f32).reshape(-1, 1),
        "lc_im": np.asarray(inp["L0_attn_lc_im"], f32).reshape(-1, 1),
        "lam_a": C["lam_a"], "lam_b": C["lam_b"],
        "tabP": C["tabP"], "tabQ": C["tabQ"], "tabU": C["tabU"], "tabV": C["tabV"],
        "WyT": asc(np.asarray(inp["L0_attn_Wy"], f32).T),
        "WyaT": asc(np.asarray(inp["L0_attn_Wya"], f32).T),
        "by": np.asarray(inp["L0_attn_by"], f32).reshape(-1, 1),
        "bya": np.asarray(inp["L0_attn_bya"], f32).reshape(-1, 1),
        "norm_w": np.asarray(inp["L0_attn_norm_w"], f32).reshape(-1, 1),
        "gate_wT": asc(np.asarray(inp["L0_ffn_gate_w"], f32).T),
        "ew1": np.asarray(inp["L0_ffn_expert_w1"], f32),
        "ew2": np.asarray(inp["L0_ffn_expert_w2"], f32),
        "identf": C["identf"], "identbf": C["identbf"],
        "lngm": np.asarray(inp["L0_ffn_ln_gamma"], f32).reshape(-1, 1),
        "lnbm": np.asarray(inp["L0_ffn_ln_beta"], f32).reshape(-1, 1),
        "WinT": asc(Win[perm, :].T),
        "WoutT": asc(np.asarray(inp["L1_attn_out_weight"], f32).T),
        "maskrep": np.tile(maskblk2, (1, 4)).astype(f32),
        "tqc": C["tab_qc"], "tqs": C["tab_qs"],
        "tkc": C["tab_kc"], "tks": C["tab_ks"],
        "mprev": C["maskTprev"], "mcur": C["maskTcur"], "permT": C["permT"],
    }
    for l in range(2):
        w2p = np.zeros((2816, D), f32)
        w2p[:2730] = w2in[l]
        base.update({
            f"wqkv{l}": asc(np.asarray(inp["L1_ffn_wqkv"], f32)[l]),
            f"wo{l}": asc(np.asarray(inp["L1_ffn_wo"], f32)[l]),
            f"qsv{l}": np.asarray(inp["L1_ffn_q_scale"], f32)[l].reshape(1, 64),
            f"ksv{l}": np.asarray(inp["L1_ffn_k_scale"], f32)[l].reshape(1, 64),
            f"lng{l}": np.asarray(inp["L1_ffn_ln_g"], f32)[l].reshape(-1, 1),
            f"lnb{l}": np.asarray(inp["L1_ffn_ln_b"], f32)[l].reshape(-1, 1),
            f"w1a{l}": asc(w1[l][:, :2730]), f"w1g{l}": asc(w1[l][:, 2730:]),
            f"b1a{l}": b1[l][:2730].reshape(-1, 1),
            f"b1g{l}": b1[l][2730:].reshape(-1, 1),
            f"w2{l}": w2p, f"b2{l}": np.asarray(inp["L1_ffn_b2"], f32)[l].reshape(-1, 1),
        })
    nc = build_fused()
    im = [dict(base, xT=asc(x[b].T)) for b in range(NB)]
    res = run_bass_kernel_spmd(nc, im, core_ids=list(range(NCORE)))
    return np.stack([asc(r["outT"].T) for r in res.results]).astype(f32)


USE_FUSED = False



def _spmd(nc, im):
    return run_bass_kernel_spmd(nc, im, core_ids=list(range(NCORE)))


def kernel(**inp):
    _ = (inp.get("L1_attn_key"), inp.get("L1_attn_value"))
    if USE_FUSED:
        return kernel_fused(inp)
    f32 = np.float32
    asc = np.ascontiguousarray
    C = make_consts()
    x = np.asarray(inp["L0_attn_x"], f32)
    xT = [asc(x[b].T) for b in range(NB)]

    base = {
        "WzT": asc(np.asarray(inp["L0_attn_Wz"], f32).T),
        "WzaT": asc(np.asarray(inp["L0_attn_Wza"], f32).T),
        "bz": np.asarray(inp["L0_attn_bz"], f32).reshape(-1, 1),
        "bza": np.asarray(inp["L0_attn_bza"], f32).reshape(-1, 1),
        "lc_re": np.asarray(inp["L0_attn_lc_re"], f32).reshape(-1, 1),
        "lc_im": np.asarray(inp["L0_attn_lc_im"], f32).reshape(-1, 1),
        "lam_a": C["lam_a"], "lam_b": C["lam_b"],
        "tabP": C["tabP"], "tabQ": C["tabQ"], "tabU": C["tabU"], "tabV": C["tabV"],
    }
    res = _spmd(build_l0h(), [dict(base, xT=xT[b]) for b in range(NB)])
    xnT = [r["xnT"] for r in res.results]
    hT = [r["hT"] for r in res.results]

    base = {
        "WyT": asc(np.asarray(inp["L0_attn_Wy"], f32).T),
        "WyaT": asc(np.asarray(inp["L0_attn_Wya"], f32).T),
        "by": np.asarray(inp["L0_attn_by"], f32).reshape(-1, 1),
        "bya": np.asarray(inp["L0_attn_bya"], f32).reshape(-1, 1),
        "norm_w": np.asarray(inp["L0_attn_norm_w"], f32).reshape(-1, 1),
    }
    res = _spmd(build_l0y(), [dict(base, hT=hT[b], xnT=xnT[b]) for b in range(NB)])
    sT = [r["sT"] for r in res.results]

    base = {
        "gate_wT": asc(np.asarray(inp["L0_ffn_gate_w"], f32).T),
        "ew1": np.asarray(inp["L0_ffn_expert_w1"], f32),
        "ew2": np.asarray(inp["L0_ffn_expert_w2"], f32),
        "identf": C["identf"],
    }
    res = _spmd(build_l0moe(), [dict(base, xnT=xnT[b]) for b in range(NB)])
    moeT = [r["moeT"] for r in res.results]

    base = {"lng": np.asarray(inp["L0_ffn_ln_gamma"], f32).reshape(-1, 1),
            "lnb": np.asarray(inp["L0_ffn_ln_beta"], f32).reshape(-1, 1)}
    res = _spmd(build_glue0(), [dict(base, moeT=moeT[b], xT=xT[b], sT=sT[b])
                                for b in range(NB)])
    x1T = [r["x1T"] for r in res.results]
    xn1T = [r["xn1T"] for r in res.results]

    Win = np.asarray(inp["L1_attn_in_proj_weight"], f32)
    perm = np.array([h * 192 + w * 64 + dd for w in range(3) for h in range(16)
                     for dd in range(64)])
    base = {"WinT": asc(Win[perm, :].T),
            "WoutT": asc(np.asarray(inp["L1_attn_out_weight"], f32).T),
            "maskrep": np.tile(C["maskblk"], (1, 4)).astype(f32),
            "identbf": C["identbf"]}
    im = []
    for c in range(NCORE):
        ps = slice(c * 1024, (c + 1) * 1024)
        arr = np.stack([xn1T[b][:, ps] for b in range(NB)], axis=-1)
        im.append(dict(base, xnp=asc(arr.reshape(D, T))))
    res = _spmd(build_l1attn(), im)
    s1T = [np.empty((D, T), f32) for _ in range(NB)]
    for c in range(NCORE):
        arr = res.results[c]["s1p"].reshape(D, 1024, NB)
        for b in range(NB):
            s1T[b][:, c * 1024:(c + 1) * 1024] = arr[:, :, b]

    w1 = np.asarray(inp["L1_ffn_w1"], f32)
    b1 = np.asarray(inp["L1_ffn_b1"], f32)
    w2in = np.asarray(inp["L1_ffn_w2"], f32)
    cur = xn1T
    out = None
    for l in range(2):
        basea = {"wqkv": asc(np.asarray(inp["L1_ffn_wqkv"], f32)[l]),
                 "wo": asc(np.asarray(inp["L1_ffn_wo"], f32)[l]),
                 "qsv": np.asarray(inp["L1_ffn_q_scale"], f32)[l].reshape(1, 64),
                 "ksv": np.asarray(inp["L1_ffn_k_scale"], f32)[l].reshape(1, 64),
                 "tqc": C["tab_qc"], "tqs": C["tab_qs"],
                 "tkc": C["tab_kc"], "tks": C["tab_ks"],
                 "mprev": C["maskTprev"], "mcur": C["maskTcur"], "permT": C["permT"]}
        res = _spmd(build_l1fa(), [dict(basea, xinT=cur[b]) for b in range(NB)])
        xmT = [r["xmidT"] for r in res.results]
        baseb = {"lng": np.asarray(inp["L1_ffn_ln_g"], f32)[l].reshape(-1, 1),
                 "lnb": np.asarray(inp["L1_ffn_ln_b"], f32)[l].reshape(-1, 1),
                 "w1a": asc(w1[l][:, :2730]), "w1g": asc(w1[l][:, 2730:]),
                 "b1a": b1[l][:2730].reshape(-1, 1), "b1g": b1[l][2730:].reshape(-1, 1)}
        res = _spmd(build_l1fb1(), [dict(baseb, xmidT=xmT[b]) for b in range(NB)])
        heT = [r["heT"] for r in res.results]
        w2p = np.zeros((2816, D), f32)
        w2p[:2730] = w2in[l]
        basec = {"w2": w2p, "b2": np.asarray(inp["L1_ffn_b2"], f32)[l].reshape(-1, 1)}
        final = l == 1
        if final:
            im = [dict(basec, heT=heT[b], xmidT=xmT[b], x1T=x1T[b], s1T=s1T[b])
                  for b in range(NB)]
        else:
            im = [dict(basec, heT=heT[b], xmidT=xmT[b]) for b in range(NB)]
        res = _spmd(build_l1fb2(final=final), im)
        cur = [r["yT"] for r in res.results]
    return np.stack([asc(cur[b].T) for b in range(NB)]).astype(f32)


def build_smoke():
    nc, a = _mk_nc(
        "smoke",
        ins={"A": ((128, 256), F32), "W": ((128, 128), F32), "b": ((128, 1), F32),
             "ident": ((128, 128), F32), "identbf": ((128, 128), BF16)},
        outs={"mmout": ((128, 256), F32), "siluout": ((128, 256), F32),
              "rowmax": ((128, 1), F32), "cums": ((128, 256), F32),
              "tpose": ((128, 128), F32), "repl": ((128, 256), F32),
              "divout": ((128, 256), F32)},
    )
    P = Prog(nc)
    at = P.sb("at", [128, 256], F32)
    wt = P.sb("wt", [128, 128], F32)
    bt = P.sb("bt", [128, 1], F32)
    it_bf = P.sb("it_bf", [128, 128], BF16)
    abf = P.sb("abf", [128, 256], BF16)
    wbf = P.sb("wbf", [128, 128], BF16)
    ones = P.sb("ones", [128, 256], F32)
    onesrow = P.sb("onesrow", [1, 128], F32)
    res = P.sb("res", [128, 256], F32)
    res2 = P.sb("res2", [128, 256], F32)
    rmax = P.sb("rmax", [128, 1], F32)
    cums = P.sb("cums", [128, 256], F32)
    tps = P.sb("tps", [128, 128], F32)
    tpbf = P.sb("tpbf", [128, 128], BF16)
    repl = P.sb("repl", [128, 256], F32)
    dv = P.sb("dv", [128, 256], F32)
    pm = P.ps("pm", [128, 512], F32)
    ptp = P.ps("ptp", [128, 128], BF16)

    P.dma("sync", at[:, :], a["A"][:, :])
    P.dma("sync", wt[:, :], a["W"][:, :])
    P.dma("sync", bt[:, :], a["b"][:, :])
    P.dma("sync", it_bf[:, :], a["identbf"][:, :])
    P.memset(ones[:, :], 1.0)
    P.memset(onesrow[:, :], 1.0)
    P.step()
    P.copy(abf[:, :], at[:, :])
    P.copy(wbf[:, :], wt[:, :], eng="vector")
    P.step()
    P.mm(pm[:, 0:256], wbf[:, :], abf[:, :], start=True, stop=True)
    P.step()
    P.ts(res[:, :], pm[:, 0:256], bt[:, 0:1], OP.add)
    P.step()
    P.dma("sync", a["mmout"][:, :], res[:, :])
    P.act(res2[:, :], pm[:, 0:256], AF.Silu, bias=bt[:, 0:1], scale=2.0)
    P.step()
    P.dma("sync", a["siluout"][:, :], res2[:, :])
    P.red(rmax[:, :], at[:, :], OP.max, negate=True)
    P.scan_cumsum(cums[:, :], ones[:, :], at[:, :], 0.0)
    P.step()
    P.dma("sync", a["rowmax"][:, :], rmax[:, :])
    P.dma("sync", a["cums"][:, :], cums[:, :])
    P.transpose(ptp[:, :], abf[:, 0:128], it_bf[:, :])
    P.step()
    P.copy(tps[:, :], ptp[:, :])
    P.step()
    P.dma("sync", a["tpose"][:, :], tps[:, :])
    P.mm(pm[:, 256:512], onesrow[0:1, :], at[0:1, :], start=True, stop=True)
    P.step()
    P.copy(repl[:, :], pm[:, 256:512], eng="vector")
    P.step()
    P.act(dv[:, :], repl[:, :], AF.Abs)
    P.step()
    P.act(dv[:, :], dv[:, :], AF.Ln)
    P.step()
    P.act(dv[:, :], dv[:, :], AF.Exp, scale=-1.0)
    P.step()
    P.tt(dv[:, :], at[:, :], dv[:, :], OP.mult)
    P.step()
    P.dma("sync", a["repl"][:, :], repl[:, :])
    P.dma("sync", a["divout"][:, :], dv[:, :])
    P.finish()
    return nc


def run_smoke():
    rng = np.random.default_rng(0)
    A = rng.standard_normal((128, 256), dtype=np.float32)
    W = rng.standard_normal((128, 128), dtype=np.float32)
    b = rng.standard_normal((128, 1), dtype=np.float32)
    ident = np.eye(128, dtype=np.float32)
    import ml_dtypes
    identbf = ident.astype(ml_dtypes.bfloat16)
    nc = build_smoke()
    im = {"A": A, "W": W, "b": b, "ident": ident, "identbf": identbf}
    res = run_bass_kernel_spmd(nc, [im], core_ids=[0])
    r = res.results[0]
    Abf = A.astype(ml_dtypes.bfloat16).astype(np.float32)
    Wbf = W.astype(ml_dtypes.bfloat16).astype(np.float32)
    mm = Wbf.T @ Abf
    def cmp(name, got, want, tol=2e-2):
        err = np.abs(got - want).max() / (np.abs(want).max() + 1e-9)
        print(f"{name:10s} relmaxerr={err:.2e} {'OK' if err < tol else 'FAIL'}")
    cmp("mmout", r["mmout"], mm + b)
    cmp("siluout", r["siluout"], (2*(mm)+b) / (1+np.exp(-(2*mm+b))))
    cmp("rowmax", r["rowmax"], -A.max(1, keepdims=True))
    cmp("cums", r["cums"], np.cumsum(A, axis=1), tol=1e-5)
    cmp("tpose", r["tpose"], Abf[:, :128].T)
    cmp("repl", r["repl"], np.broadcast_to(A[0:1, :], (128, 256)))
    cmp("divout", r["divout"], A / np.abs(A[0:1, :]), tol=1e-4)


if __name__ == "__main__":
    run_smoke()


# version 04 (final)
# speedup vs baseline: 1.0700x; 1.0700x over previous
import math
import numpy as np
from contextlib import ExitStack

import concourse.bass as bass
import concourse.mybir as mybir
from concourse.bass_utils import run_bass_kernel_spmd

F32 = mybir.dt.float32
BF16 = mybir.dt.bfloat16
AF = mybir.ActivationFunctionType
OP = mybir.AluOpType
AX = mybir.AxisListType

D = 1024
T = 8192
NB = 8
NCORE = 8
KT = D // 128
CH = 512
NCHUNK = T // CH

ENGS = ("tensor", "vector", "scalar", "gpsimd", "sync")


class Prog:

    def __init__(self, nc):
        self.nc = nc
        self.st = ExitStack()
        self.clk = self.st.enter_context(nc.semaphore("clk"))
        self.dma_sem = self.st.enter_context(nc.semaphore("dmasem"))
        self.clk_total = 0
        self.dma_total = 0
        self.active = {}
        self.step_dmas = 0
        self.wait_tgt = (0, 0)
        self.eng_waited = {e: (0, 0) for e in ENGS}

    def eng(self, name):
        return getattr(self.nc, name)

    def _pre(self, name):
        w = self.eng_waited[name]
        tgt = self.wait_tgt
        e = self.eng(name)
        if w[0] < tgt[0]:
            e.wait_ge(self.clk, tgt[0])
        if w[1] < tgt[1]:
            e.wait_ge(self.dma_sem, 16 * tgt[1])
        self.eng_waited[name] = tgt

    def emit(self, name, fn):
        self._pre(name)
        inst = fn(self.eng(name))
        self.active[name] = inst
        return inst

    def dma(self, q, out, in_):
        self._pre(q)
        inst = self.eng(q).dma_start(out=out, in_=in_)
        inst.then_inc(self.dma_sem, 16)
        self.step_dmas += 1
        if q not in self.active:
            self.active[q] = None
        return inst

    def dma_def(self, q, out, in_, sem):
        self._pre(q)
        inst = self.eng(q).dma_start(out=out, in_=in_)
        inst.then_inc(sem, 16)
        if q not in self.active:
            self.active[q] = None
        return inst

    def step(self):
        for name, inst in self.active.items():
            if inst is not None:
                inst.then_inc(self.clk, 1)
                self.clk_total += 1
        self.dma_total += self.step_dmas
        self.active = {}
        self.step_dmas = 0
        self.wait_tgt = (self.clk_total, self.dma_total)

    def finish(self):
        self.step()
        e = self.eng("sync")
        e.wait_ge(self.clk, self.clk_total)
        e.wait_ge(self.dma_sem, 16 * self.dma_total)

    def mm(self, out, lhsT, rhs, start, stop):
        return self.emit("tensor", lambda e: e.matmul(out, lhsT, rhs, start=start, stop=stop))

    def transpose(self, out, in_, ident):
        return self.emit("tensor", lambda e: e.transpose(out, in_, ident))

    def tt(self, out, a, b, op, eng="vector"):
        return self.emit(eng, lambda e: e.tensor_tensor(out=out, in0=a, in1=b, op=op))

    def ts(self, out, a, s1, op, eng="vector"):
        return self.emit(eng, lambda e: e.tensor_scalar(out=out, in0=a, scalar1=s1, scalar2=None, op0=op))

    def ts2(self, out, a, s1, s2, op0, op1, eng="vector"):
        return self.emit(eng, lambda e: e.tensor_scalar(out=out, in0=a, scalar1=s1, scalar2=s2, op0=op0, op1=op1))

    def act(self, out, in_, func, bias=None, scale=None):
        kw = {}
        if bias is not None:
            kw["bias"] = bias
        if scale is not None:
            kw["scale"] = scale
        return self.emit("scalar", lambda e: e.activation(out, in_, func, **kw))

    def copy(self, out, in_, eng="scalar"):
        if eng == "scalar":
            return self.emit(eng, lambda e: e.copy(out, in_))
        return self.emit(eng, lambda e: e.tensor_copy(out=out, in_=in_))

    def red(self, out, in_, op, negate=False, eng="vector"):
        return self.emit(eng, lambda e: e.tensor_reduce(out=out, in_=in_, axis=AX.X, op=op, negate=negate))

    def scan_cumsum(self, out, ones, data, init):
        return self.emit("vector", lambda e: e.tensor_tensor_scan(
            out=out, data0=ones, data1=data, initial=init, op0=OP.mult, op1=OP.add))

    def memset(self, ap, c, eng="vector"):
        return self.emit(eng, lambda e: e.memset(ap, c))

    def scope(self):
        from contextlib import contextmanager

        @contextmanager
        def _s():
            old = self.st
            self.st = ExitStack()
            try:
                yield
            finally:
                self.st.close()
                self.st = old
        return _s()

    def sb(self, name, shape, dtype):
        self._n = getattr(self, "_n", 0) + 1
        return self.st.enter_context(self.nc.sbuf_tensor(f"sb{self._n}_{name}", shape, dtype))

    def ps(self, name, shape, dtype=F32):
        self._n = getattr(self, "_n", 0) + 1
        return self.st.enter_context(self.nc.psum_tensor(f"ps{self._n}_{name}", shape, dtype))


def _mk_nc(name, ins, outs):
    nc = bass.Bass("TRN2", target_bir_lowering=False)
    aps = {}
    for n, (shp, dt) in ins.items():
        aps[n] = nc.dram_tensor(n, list(shp), dt, kind="ExternalInput").ap()
    for n, (shp, dt) in outs.items():
        aps[n] = nc.dram_tensor(n, list(shp), dt, kind="ExternalOutput").ap()
    return nc, aps



def make_consts():
    import ml_dtypes
    C = {}
    d = np.arange(D, dtype=np.float64) / (D - 1)
    th = 0.001 ** d
    s = np.arange(T, dtype=np.float64)
    ph = th[:, None] * s[None, :]
    eneg = np.exp(-1e-5 * s)[None, :]
    epos = np.exp(1e-5 * s)[None, :]
    C["tabP"] = (eneg * np.cos(ph)).astype(np.float32)
    C["tabQ"] = (eneg * np.sin(ph)).astype(np.float32)
    C["tabU"] = (epos * np.cos(ph)).astype(np.float32)
    C["tabV"] = (epos * np.sin(ph)).astype(np.float32)
    lam = np.exp(1e-5)
    C["lam_a"] = (lam * np.cos(th)).astype(np.float32).reshape(D, 1)
    C["lam_b"] = (lam * np.sin(th)).astype(np.float32).reshape(D, 1)
    hi = np.arange(0, 64, 2, dtype=np.float64)
    invf = 1.0 / 10000.0 ** (hi / 64.0)
    t = np.arange(T, dtype=np.float64)
    fr = np.concatenate([t[:, None] * invf, t[:, None] * invf], 1)
    base = (hi + 0.4 * 64) / (1.4 * 64)
    power = (t - T // 2) / 256.0
    scl = np.concatenate([base[None, :] ** power[:, None]] * 2, 1)
    cos, sin = np.cos(fr), np.sin(fr)
    mk = lambda a: np.vstack([a.T, a.T]).astype(np.float32)
    C["tab_qc"] = mk(cos * scl * 8.0)
    C["tab_qs"] = mk(sin * scl * 8.0)
    C["tab_kc"] = mk(cos / scl)
    C["tab_ks"] = mk(sin / scl)
    jj = np.arange(512)[:, None]
    ii = np.arange(512)[None, :]
    C["maskTprev"] = np.where(jj >= ii, 0.0, -1e30).astype(np.float32)
    C["maskTcur"] = np.where(jj <= ii, 0.0, -1e30).astype(np.float32)
    r = np.arange(128)
    C["maskblk"] = np.where((r[:, None] // 8) == (r[None, :] // 8), 0.0, -1e30).astype(np.float32)
    P64 = np.zeros((64, 64))
    for rr in range(32):
        P64[rr, rr + 32] = -1.0
        P64[rr + 32, rr] = 1.0
    Perm = np.zeros((128, 128))
    Perm[:64, :64] = P64
    Perm[64:, 64:] = P64
    C["permT"] = Perm.T.astype(ml_dtypes.bfloat16)
    C["identbf"] = np.eye(128).astype(ml_dtypes.bfloat16)
    C["identf"] = np.eye(128, dtype=np.float32)
    return C



def load_wbf(P, ap, kt, mtot, stage, tiles):
    for k in range(kt):
        P.dma("sync", stage[:, 0:mtot], ap[k * 128:(k + 1) * 128, :])
        P.step()
        P.copy(tiles[k][:, 0:mtot], stage[:, 0:mtot])
        P.step()


def load_pvec(P, ap, tiles):
    for k in range(len(tiles)):
        P.dma("sync", tiles[k][:, :], ap[k * 128:(k + 1) * 128, :])
    P.step()


def ln_rows(P, tiles, n, sq, psum, ones_col, ones_row, rstd_row, nm_row,
            rep_a, rep_b, eps, pr):
    for k in range(8):
        P.emit("scalar", lambda e, k=k: e.activation(sq[:, :n], tiles[k][:, :n], AF.Square))
        P.step()
        P.mm(psum[0:1, 0:n], ones_col[:, :], tiles[k][:, :n], start=(k == 0), stop=(k == 7))
        P.mm(pr[0:1, 0:n], ones_col[:, :], sq[:, :n], start=(k == 0), stop=(k == 7))
        P.step()
    P.ts(rstd_row[0:1, :n], pr[0:1, 0:n], 1.0 / D, OP.mult)
    P.ts(nm_row[0:1, :n], psum[0:1, 0:n], 1.0 / D, OP.mult)
    P.step()
    P.tt(pr[0:1, 0:n], nm_row[0:1, :n], nm_row[0:1, :n], OP.mult)
    P.step()
    P.tt(rstd_row[0:1, :n], rstd_row[0:1, :n], pr[0:1, 0:n], OP.subtract)
    P.step()
    P.ts(rstd_row[0:1, :n], rstd_row[0:1, :n], float(eps), OP.add)
    P.step()
    P.act(rstd_row[0:1, :n], rstd_row[0:1, :n], AF.Ln)
    P.step()
    P.act(rstd_row[0:1, :n], rstd_row[0:1, :n], AF.Exp, scale=-0.5)
    P.step()
    P.tt(nm_row[0:1, :n], nm_row[0:1, :n], rstd_row[0:1, :n], OP.mult)
    P.step()
    P.ts(nm_row[0:1, :n], nm_row[0:1, :n], -1.0, OP.mult)
    P.step()
    for off in range(0, n, 512):
        w = min(512, n - off)
        P.mm(psum[:, 0:w], ones_row[0:1, :], rstd_row[0:1, off:off + w], start=True, stop=True)
        P.mm(pr[:, 0:w], ones_row[0:1, :], nm_row[0:1, off:off + w], start=True, stop=True)
        P.step()
        P.copy(rep_a[:, off:off + w], psum[:, 0:w], eng="vector")
        P.copy(rep_b[:, off:off + w], pr[:, 0:w])
        P.step()



def stage_l0h(P, a):
  with P.scope():
      CHL = 512
      wz = [P.sb(f"wz{k}", [128, D], BF16) for k in range(8)]
      wza = [P.sb(f"wza{k}", [128, D], BF16) for k in range(8)]
      stage = P.sb("stage", [128, D], F32)
      bz_t = [P.sb(f"bz{k}", [128, 1], F32) for k in range(8)]
      bza_t = [P.sb(f"bza{k}", [128, 1], F32) for k in range(8)]
      cc = [P.sb(f"cc{k}", [128, 1], F32) for k in range(8)]
      cs = [P.sb(f"cs{k}", [128, 1], F32) for k in range(8)]
      tmpc = [P.sb(f"tc{k}", [128, 1], F32) for k in range(2)]
      ones_col = P.sb("onescol", [128, 1], F32)
      ones_row = P.sb("onesrow", [1, 128], F32)
      ones_ch = P.sb("onesch", [128, CHL], F32)
      x_t = [P.sb(f"x{k}", [128, CHL], F32) for k in range(8)]
      xnbf = [P.sb(f"xnbf{k}", [128, CHL], BF16) for k in range(8)]
      z_t = [P.sb(f"z{k}", [128, CHL], F32) for k in range(8)]
      tab1 = [P.sb(f"tb1{k}", [128, CHL], F32) for k in range(8)]
      tab2 = [P.sb(f"tb2{k}", [128, CHL], F32) for k in range(8)]
      Sc = [P.sb(f"scn{k}", [128, CHL], F32) for k in range(8)]
      Ss = [P.sb(f"ssn{k}", [128, CHL], F32) for k in range(8)]
      tmp = P.sb("tmp", [128, CHL], F32)
      sq = P.sb("sq", [128, CHL], F32)
      rep_a = P.sb("repa", [128, CHL], F32)
      rep_b = P.sb("repb", [128, CHL], F32)
      rstd_row = P.sb("rstd", [1, CHL], F32)
      nm_row = P.sb("nm", [1, CHL], F32)
      pa = P.ps("pa", [128, 512])
      pb = P.ps("pb", [128, 512])
      pr = P.ps("pr", [128, 512])

      P.memset(ones_col[:, :], 1.0)
      P.memset(ones_row[:, :], 1.0)
      P.memset(ones_ch[:, :], 1.0)
      load_pvec(P, a["bz"], bz_t)
      load_pvec(P, a["bza"], bza_t)
      for k in range(8):
          sl = slice(k * 128, (k + 1) * 128)
          P.dma("sync", cc[k][:, :], a["lc_re"][sl, :])
          P.dma("sync", cs[k][:, :], a["lc_im"][sl, :])
          P.dma("sync", tmpc[0][:, :], a["lam_a"][sl, :])
          P.dma("sync", tmpc[1][:, :], a["lam_b"][sl, :])
          P.step()
          P.tt(sq[:, 0:1], cc[k][:, :], tmpc[0][:, :], OP.mult)
          P.tt(tmp[:, 0:1], cs[k][:, :], tmpc[1][:, :], OP.mult, eng="gpsimd")
          P.step()
          P.tt(sq[:, 1:2], cc[k][:, :], tmpc[1][:, :], OP.mult)
          P.tt(tmp[:, 1:2], cs[k][:, :], tmpc[0][:, :], OP.mult, eng="gpsimd")
          P.step()
          P.tt(cc[k][:, :], sq[:, 0:1], tmp[:, 0:1], OP.subtract)
          P.step()
          P.tt(cs[k][:, :], sq[:, 1:2], tmp[:, 1:2], OP.add)
          P.step()
          P.ts(cs[k][:, :], cs[k][:, :], -1.0, OP.mult)
          P.step()
      load_wbf(P, a["WzT"], 8, D, stage, wz)
      load_wbf(P, a["WzaT"], 8, D, stage, wza)

      for c in range(T // CHL):
          c0 = c * CHL
          sl = slice(c0, c0 + CHL)
          for k in range(8):
              P.dma("sync", x_t[k][:, :], a["xT"][k * 128:(k + 1) * 128, sl])
              P.dma("gpsimd", tab1[k][:, :], a["tabP"][k * 128:(k + 1) * 128, sl])
              P.dma("gpsimd", tab2[k][:, :], a["tabQ"][k * 128:(k + 1) * 128, sl])
          P.step()
          ln_rows(P, x_t, CHL, sq, pa, ones_col, ones_row, rstd_row, nm_row,
                  rep_a, rep_b, 1e-5, pr)
          for k in range(8):
              P.tt(x_t[k][:, :], x_t[k][:, :], rep_a[:, :], OP.mult)
              P.step()
              P.tt(x_t[k][:, :], x_t[k][:, :], rep_b[:, :], OP.add)
              P.step()
              P.copy(xnbf[k][:, :], x_t[k][:, :])
              P.dma("sync", a["xnT"][k * 128:(k + 1) * 128, sl], x_t[k][:, :])
              P.step()
              if "xnbfT" in a:
                  P.dma("sync", a["xnbfT"][k * 128:(k + 1) * 128, sl], xnbf[k][:, :])
          for o in range(8):
              for k in range(8):
                  P.mm(pa[:, :], wz[k][:, o * 128:(o + 1) * 128], xnbf[k][:, :],
                       start=(k == 0), stop=(k == 7))
                  P.mm(pb[:, :], wza[k][:, o * 128:(o + 1) * 128], xnbf[k][:, :],
                       start=(k == 0), stop=(k == 7))
              P.step()
              P.act(tmp[:, :], pb[:, :], AF.Silu, bias=bza_t[o][:, 0:1])
              P.step()
              P.ts(z_t[o][:, :], pa[:, :], bz_t[o][:, 0:1], OP.add)
              P.step()
              P.tt(z_t[o][:, :], z_t[o][:, :], tmp[:, :], OP.mult)
              P.step()
          for k in range(8):
              P.tt(tmp[:, :], z_t[k][:, :], tab1[k][:, :], OP.mult)
              P.step()
              P.scan_cumsum(Sc[k][:, :], ones_ch[:, :], tmp[:, :], cc[k][:, 0:1])
              P.step()
              P.copy(cc[k][:, 0:1], Sc[k][:, CHL - 1:CHL], eng="vector")
              P.tt(tmp[:, :], z_t[k][:, :], tab2[k][:, :], OP.mult, eng="gpsimd")
              P.step()
              P.scan_cumsum(Ss[k][:, :], ones_ch[:, :], tmp[:, :], cs[k][:, 0:1])
              P.step()
              P.copy(cs[k][:, 0:1], Ss[k][:, CHL - 1:CHL], eng="vector")
              P.step()
          for k in range(8):
              P.dma("sync", tab1[k][:, :], a["tabU"][k * 128:(k + 1) * 128, sl])
              P.dma("gpsimd", tab2[k][:, :], a["tabV"][k * 128:(k + 1) * 128, sl])
          P.step()
          for k in range(8):
              P.tt(Sc[k][:, :], Sc[k][:, :], tab1[k][:, :], OP.mult)
              P.tt(Ss[k][:, :], Ss[k][:, :], tab2[k][:, :], OP.mult, eng="gpsimd")
              P.step()
              P.tt(Sc[k][:, :], Sc[k][:, :], Ss[k][:, :], OP.add)
              P.step()
              P.dma("sync", a["hT"][k * 128:(k + 1) * 128, sl], Sc[k][:, :])
          P.step()


def build_l0h():
    nc, a = _mk_nc(
        "l0h",
        ins={"xT": ((D, T), F32), "WzT": ((D, D), F32), "WzaT": ((D, D), F32),
             "bz": ((D, 1), F32), "bza": ((D, 1), F32),
             "lc_re": ((D, 1), F32), "lc_im": ((D, 1), F32),
             "lam_a": ((D, 1), F32), "lam_b": ((D, 1), F32),
             "tabP": ((D, T), F32), "tabQ": ((D, T), F32),
             "tabU": ((D, T), F32), "tabV": ((D, T), F32)},
        outs={"xnT": ((D, T), F32), "hT": ((D, T), F32), "xnbfT": ((D, T), BF16)},
    )
    P = Prog(nc)
    stage_l0h(P, a)
    P.finish()
    return nc


def stage_l0y(P, a):
  with P.scope():
      CHL = 512
      wy = [P.sb(f"wy{k}", [128, D], BF16) for k in range(8)]
      wya = [P.sb(f"wya{k}", [128, D], BF16) for k in range(8)]
      stage = P.sb("stage", [128, D], F32)
      by_t = [P.sb(f"by{k}", [128, 1], F32) for k in range(8)]
      bya_t = [P.sb(f"bya{k}", [128, 1], F32) for k in range(8)]
      nw_t = [P.sb(f"nw{k}", [128, 1], F32) for k in range(8)]
      ones_col = P.sb("onescol", [128, 1], F32)
      ones_row = P.sb("onesrow", [1, 128], F32)
      h_t = [P.sb(f"h{k}", [128, CHL], F32) for k in range(8)]
      nbf = [P.sb(f"nbf{k}", [128, CHL], BF16) for k in range(8)]
      xbf = [P.sb(f"xbf{k}", [128, CHL], BF16) for k in range(8)]
      sq = P.sb("sq", [128, CHL], F32)
      tmp = P.sb("tmp", [128, CHL], F32)
      sout = P.sb("sout", [128, CHL], F32)
      rep_a = P.sb("repa", [128, CHL], F32)
      rrow = P.sb("rrow", [1, CHL], F32)
      pa = P.ps("pa", [128, 512])
      pb = P.ps("pb", [128, 512])

      P.memset(ones_col[:, :], 1.0)
      P.memset(ones_row[:, :], 1.0)
      load_pvec(P, a["by"], by_t)
      load_pvec(P, a["bya"], bya_t)
      load_pvec(P, a["norm_w"], nw_t)
      load_wbf(P, a["WyT"], 8, D, stage, wy)
      load_wbf(P, a["WyaT"], 8, D, stage, wya)

      for c in range(T // CHL):
          sl = slice(c * CHL, c * CHL + CHL)
          for k in range(8):
              P.dma("sync", h_t[k][:, :], a["hT"][k * 128:(k + 1) * 128, sl])
          P.step()
          for k in range(8):
              P.emit("scalar", lambda e, k=k: e.activation(sq[:, :], h_t[k][:, :], AF.Square))
              P.step()
              P.mm(pa[0:1, :], ones_col[:, :], sq[:, :], start=(k == 0), stop=(k == 7))
              P.step()
          P.ts2(rrow[0:1, :], pa[0:1, :], 1.0 / D, 1e-6, OP.mult, OP.add)
          P.step()
          P.act(rrow[0:1, :], rrow[0:1, :], AF.Ln)
          P.step()
          P.act(rrow[0:1, :], rrow[0:1, :], AF.Exp, scale=-0.5)
          P.step()
          P.mm(pa[:, :], ones_row[0:1, :], rrow[0:1, :], start=True, stop=True)
          P.step()
          P.copy(rep_a[:, :], pa[:, :], eng="vector")
          P.step()
          for k in range(8):
              P.tt(tmp[:, :], h_t[k][:, :], rep_a[:, :], OP.mult)
              P.step()
              P.ts(tmp[:, :], tmp[:, :], nw_t[k][:, 0:1], OP.mult)
              P.step()
              P.copy(nbf[k][:, :], tmp[:, :])
              P.step()
          for k in range(8):
              P.dma("sync", h_t[k][:, :], a["xnT"][k * 128:(k + 1) * 128, sl])
          P.step()
          for k in range(8):
              P.copy(xbf[k][:, :], h_t[k][:, :])
          P.step()
          for o in range(8):
              for k in range(8):
                  P.mm(pa[:, :], wy[k][:, o * 128:(o + 1) * 128], nbf[k][:, :],
                       start=(k == 0), stop=(k == 7))
                  P.mm(pb[:, :], wya[k][:, o * 128:(o + 1) * 128], xbf[k][:, :],
                       start=(k == 0), stop=(k == 7))
              P.step()
              P.act(tmp[:, :], pb[:, :], AF.Silu, bias=bya_t[o][:, 0:1])
              P.step()
              P.ts(sout[:, :], pa[:, :], by_t[o][:, 0:1], OP.add)
              P.step()
              P.tt(sout[:, :], sout[:, :], tmp[:, :], OP.mult)
              P.step()
              P.dma("sync", a["sT"][o * 128:(o + 1) * 128, sl], sout[:, :])
              P.step()


def build_l0y():
    nc, a = _mk_nc(
        "l0y",
        ins={"hT": ((D, T), F32), "xnT": ((D, T), F32),
             "WyT": ((D, D), F32), "WyaT": ((D, D), F32),
             "by": ((D, 1), F32), "bya": ((D, 1), F32), "norm_w": ((D, 1), F32)},
        outs={"sT": ((D, T), F32)},
    )
    P = Prog(nc)
    stage_l0y(P, a)
    P.finish()
    return nc


def stage_l0moe(P, a):
  with P.scope():
    nc = P.nc
    CHM = 1024
    NS = CHM // 512
    stage = P.sb("stage", [128, 2048], F32)
    sbf = P.sb("sbf", [128, 2048], BF16)
    xnbf = [P.sb(f"xnbf{k}", [128, CHM], BF16) for k in range(8)]
    gw = [P.sb(f"gw{k}", [128, 8], F32) for k in range(8)]
    identf = P.sb("idf", [128, 128], F32)
    gT = P.sb("gT", [8, CHM], F32)
    gN = P.sb("gN", [128, 8], F32)
    e_t = P.sb("e_t", [128, 8], F32)
    negmax = P.sb("negmax", [128, 1], F32)
    ssum = P.sb("ssum", [128, 1], F32)
    tvf = P.sb("tvf", [128, 1], F32)
    mN = P.sb("mN", [128, 8], F32)
    mT = P.sb("mT", [1, 8 * CHM], BF16)
    tvrow = P.sb("tvrow", [1, CHM], F32)
    m_bf = P.sb("m_bf", [128, CHM], BF16)
    xe = [P.sb(f"xe{k}", [128, 512], BF16) for k in range(8)]
    w1sb = [P.sb(f"w1sb{k}", [128, 2048], BF16) for k in range(8)]
    w2sb = [P.sb(f"w2sb{k}", [128, 1024], BF16) for k in range(16)]
    he = [[P.sb(f"he{n}_{k}", [128, 512], BF16) for k in range(16)] for n in range(NS)]
    acc = [P.sb(f"acc{k}", [128, CHM], BF16) for k in range(8)]
    tmp_bf = [P.sb(f"tmpbf{i}", [128, 512], BF16) for i in range(2)]
    repf = P.sb("repf", [128, 512], F32)
    outb = P.sb("outb", [128, 512], F32)
    xg = [P.sb(f"xg{i}", [128, 512], F32) for i in range(2)]
    ones_row = P.sb("onesrow", [1, 128], F32)
    ones_row_bf = P.sb("onesrowbf", [1, 128], BF16)
    pa = [P.ps("pa0", [128, 512]), P.ps("pa1", [128, 512])]
    pb = [P.ps("pb0", [128, 512]), P.ps("pb1", [128, 512])]
    pt = P.ps("pt", [128, 128])
    w1sem = P.st.enter_context(nc.semaphore("w1sem"))
    w2sem = P.st.enter_context(nc.semaphore("w2sem"))
    w1cnt = [0]
    w2cnt = [0]

    def fetch_w1(e):
        for k in range(8):
            P.dma_def("scalar", w1sb[k][:, :], a["w1s"][e, k * 128:(k + 1) * 128, :], w1sem)
            w1cnt[0] += 1

    def fetch_w2(e):
        for k in range(16):
            P.dma_def("scalar", w2sb[k][:, :], a["w2s"][e, k * 128:(k + 1) * 128, :], w2sem)
            w2cnt[0] += 1

    P.memset(ones_row[:, :], 1.0)
    P.memset(ones_row_bf[:, :], 1.0, eng="gpsimd")
    P.dma("sync", identf[:, :], a["identf"][:, :])
    for k in range(8):
        P.dma("sync", stage[:, 0:8], a["gate_wT"][k * 128:(k + 1) * 128, :])
        P.step()
        P.copy(gw[k][:, :], stage[:, 0:8], eng="vector")
        P.step()
    for e in range(8):
        for kt in range(8):
            P.dma("sync", stage[:, :], a["ew1"][e, kt * 128:(kt + 1) * 128, :])
            P.step()
            P.copy(sbf[:, :], stage[:, :])
            P.step()
            P.dma("sync", a["w1s"][e, kt * 128:(kt + 1) * 128, :], sbf[:, :])
            P.step()
        for kt in range(16):
            P.dma("sync", stage[:, 0:1024], a["ew2"][e, kt * 128:(kt + 1) * 128, :])
            P.step()
            P.copy(sbf[:, 0:1024], stage[:, 0:1024])
            P.step()
            P.dma("sync", a["w2s"][e, kt * 128:(kt + 1) * 128, :], sbf[:, 0:1024])
            P.step()
    for c in range(T // CHM):
        sl = slice(c * CHM, c * CHM + CHM)
        for k in range(8):
            P.dma("sync", xnbf[k][:, :], a["xnbfT"][k * 128:(k + 1) * 128, sl])
            P.memset(acc[k][:, :], 0.0, eng=("vector" if k % 2 == 0 else "gpsimd"))
        fetch_w1(0)
        fetch_w2(0)
        for sub in range(NS):
            for k in range(8):
                P.dma("sync", xg[k % 2][:, :],
                      a["xnT"][k * 128:(k + 1) * 128,
                               c * CHM + sub * 512:c * CHM + (sub + 1) * 512])
                P.step()
                P.mm(pa[0][0:8, :], gw[k][:, :], xg[k % 2][:, :],
                     start=(k == 0), stop=(k == 7))
            P.step()
            P.copy(gT[:, sub * 512:(sub + 1) * 512], pa[0][0:8, :], eng="vector")
            P.step()
        for j in range(CHM // 128):
            jsl = slice(j * 128, (j + 1) * 128)
            P.transpose(pt[:, 0:8], gT[:, jsl], identf[0:8, 0:8])
            P.step()
            P.copy(gN[:, :], pt[:, 0:8], eng="vector")
            P.step()
            P.red(negmax[:, :], gN[:, :], OP.max, negate=True)
            P.step()
            P.act(e_t[:, :], gN[:, :], AF.Exp, bias=negmax[:, 0:1])
            P.step()
            P.red(ssum[:, :], e_t[:, :], OP.add)
            P.step()
            P.ts2(mN[:, :], gN[:, :], -1.0, negmax[:, 0:1], OP.mult, OP.is_le)
            P.step()
            P.ts2(tvf[:, :], ssum[:, :], 1e-8, 1.0, OP.mult, OP.add)
            P.step()
            P.act(tvf[:, :], tvf[:, :], AF.Ln)
            P.step()
            P.act(tvf[:, :], tvf[:, :], AF.Exp, scale=-1.0)
            P.step()
            for e in range(8):
                dst = pa[0] if e < 4 else pa[1]
                P.transpose(dst[0:1, (e % 4) * 128:(e % 4 + 1) * 128], mN[:, e:e + 1],
                            identf[:, :])
            P.step()
            for e in range(8):
                srcp = pa[0] if e < 4 else pa[1]
                P.copy(mT[0:1, e * CHM + j * 128: e * CHM + (j + 1) * 128],
                       srcp[0:1, (e % 4) * 128:(e % 4 + 1) * 128],
                       eng="vector")
            P.step()
            P.transpose(pt[0:1, 0:128], tvf[:, :], identf[:, :])
            P.step()
            P.copy(tvrow[0:1, jsl], pt[0:1, 0:128], eng="vector")
            P.step()
        for e in range(8):
            P.eng("tensor").wait_ge(w1sem, 16 * w1cnt[0])
            for sub in range(NS):
                P.mm(pa[0][:, :], ones_row_bf[0:1, :],
                     mT[0:1, e * CHM + sub * 512: e * CHM + (sub + 1) * 512],
                     start=True, stop=True)
                P.step()
                P.copy(m_bf[:, sub * 512:(sub + 1) * 512], pa[0][:, :])
                P.step()
            for nsub in range(NS):
                nsl = slice(nsub * 512, (nsub + 1) * 512)
                for k in range(8):
                    P.tt(xe[k][:, :], xnbf[k][:, nsl], m_bf[:, nsl], OP.mult,
                         eng=("vector" if k % 2 == 0 else "gpsimd"))
                P.step()
                for ht in range(16):
                    for k in range(8):
                        P.mm(pa[ht % 2][:, :], w1sb[k][:, ht * 128:(ht + 1) * 128],
                             xe[k][:, :], start=(k == 0), stop=(k == 7))
                    if ht > 0:
                        P.act(he[nsub][ht - 1][:, :], pa[(ht - 1) % 2][:, :], AF.Gelu)
                    P.step()
                P.act(he[nsub][15][:, :], pa[1][:, :], AF.Gelu)
                P.step()
            if e < 7:
                fetch_w1(e + 1)
            P.eng("tensor").wait_ge(w2sem, 16 * w2cnt[0])
            jobs = [(nsub, ot) for nsub in range(NS) for ot in range(8)]
            for idx, (nsub, ot) in enumerate(jobs):
                par = idx % 2
                for h in range(16):
                    P.mm(pb[par][:, :], w2sb[h][:, ot * 128:(ot + 1) * 128],
                         he[nsub][h][:, :], start=(h == 0), stop=(h == 15))
                if idx > 0:
                    P.copy(tmp_bf[1 - par][:, :], pb[1 - par][:, :])
                if idx > 1:
                    pn, po_ = jobs[idx - 2]
                    P.tt(acc[po_][:, pn * 512:(pn + 1) * 512],
                         acc[po_][:, pn * 512:(pn + 1) * 512], tmp_bf[par][:, :],
                         OP.add)
                P.step()
            pn, po_ = jobs[-1]
            P.copy(tmp_bf[1][:, :], pb[1][:, :])
            pn2, po2 = jobs[-2]
            P.tt(acc[po2][:, pn2 * 512:(pn2 + 1) * 512],
                 acc[po2][:, pn2 * 512:(pn2 + 1) * 512], tmp_bf[0][:, :], OP.add)
            P.step()
            P.tt(acc[po_][:, pn * 512:(pn + 1) * 512],
                 acc[po_][:, pn * 512:(pn + 1) * 512], tmp_bf[1][:, :], OP.add)
            P.step()
            if e < 7:
                fetch_w2(e + 1)
        for sub in range(NS):
            nsl = slice(sub * 512, (sub + 1) * 512)
            P.mm(pa[0][:, :], ones_row[0:1, :], tvrow[0:1, nsl], start=True, stop=True)
            P.step()
            P.copy(repf[:, :], pa[0][:, :], eng="vector")
            P.step()
            for ot in range(8):
                P.tt(outb[:, :], acc[ot][:, nsl], repf[:, :], OP.mult)
                P.step()
                P.dma("sync", a["moeT"][ot * 128:(ot + 1) * 128,
                                        c * CHM + sub * 512: c * CHM + (sub + 1) * 512],
                      outb[:, :])
                P.step()


def build_l0moe():
    nc, a = _mk_nc(
        "l0moe",
        ins={"xnT": ((D, T), F32), "xnbfT": ((D, T), BF16),
             "gate_wT": ((D, 8), F32),
             "ew1": ((8, D, 2048), F32), "ew2": ((8, 2048, D), F32),
             "identf": ((128, 128), F32)},
        outs={"moeT": ((D, T), F32)},
    )
    a["w1s"] = nc.dram_tensor("w1s", [8, D, 2048], BF16, kind="Internal").ap()
    a["w2s"] = nc.dram_tensor("w2s", [8, 2048, D], BF16, kind="Internal").ap()
    P = Prog(nc)
    stage_l0moe(P, a)
    P.finish()
    return nc


def stage_glue0(P, a):
  with P.scope():
      CHL = 512
      agbf = P.sb("agbf", [128, 512], BF16) if "agin" in a else None
      lng_t = [P.sb(f"lng{k}", [128, 1], F32) for k in range(8)]
      lnb_t = [P.sb(f"lnb{k}", [128, 1], F32) for k in range(8)]
      ones_col = P.sb("onescol", [128, 1], F32)
      ones_row = P.sb("onesrow", [1, 128], F32)
      m_t = [P.sb(f"m{k}", [128, CHL], F32) for k in range(8)]
      x_t = [P.sb(f"x{k}", [128, CHL], F32) for k in range(8)]
      s_t = [P.sb(f"s{k}", [128, CHL], F32) for k in range(8)]
      sq = P.sb("sq", [128, CHL], F32)
      rep_a = P.sb("repa", [128, CHL], F32)
      rep_b = P.sb("repb", [128, CHL], F32)
      rstd_row = P.sb("rstd", [1, CHL], F32)
      nm_row = P.sb("nm", [1, CHL], F32)
      pa = P.ps("pa", [128, 512])
      pr = P.ps("pr", [128, 512])

      P.memset(ones_col[:, :], 1.0)
      P.memset(ones_row[:, :], 1.0)
      load_pvec(P, a["lng"], lng_t)
      load_pvec(P, a["lnb"], lnb_t)
      for c in range(T // CHL):
          sl = slice(c * CHL, c * CHL + CHL)
          for k in range(8):
              P.dma("sync", m_t[k][:, :], a["moeT"][k * 128:(k + 1) * 128, sl])
              P.dma("gpsimd", x_t[k][:, :], a["xT"][k * 128:(k + 1) * 128, sl])
              P.dma("scalar", s_t[k][:, :], a["sT"][k * 128:(k + 1) * 128, sl])
          P.step()
          ln_rows(P, m_t, CHL, sq, pa, ones_col, ones_row, rstd_row, nm_row,
                  rep_a, rep_b, 1e-5, pr)
          for k in range(8):
              P.tt(m_t[k][:, :], m_t[k][:, :], rep_a[:, :], OP.mult)
              P.step()
              P.tt(m_t[k][:, :], m_t[k][:, :], rep_b[:, :], OP.add)
              P.step()
              P.ts2(m_t[k][:, :], m_t[k][:, :], lng_t[k][:, 0:1], lnb_t[k][:, 0:1],
                    OP.mult, OP.add)
              P.step()
              P.act(m_t[k][:, :], m_t[k][:, :], AF.Silu)
              P.step()
              P.tt(m_t[k][:, :], m_t[k][:, :], s_t[k][:, :], OP.mult)
              P.step()
              P.tt(x_t[k][:, :], x_t[k][:, :], m_t[k][:, :], OP.add)
              P.step()
              P.dma("sync", a["x1T"][k * 128:(k + 1) * 128, sl], x_t[k][:, :])
          P.step()
          ln_rows(P, x_t, CHL, sq, pa, ones_col, ones_row, rstd_row, nm_row,
                  rep_a, rep_b, 1e-5, pr)
          for k in range(8):
              P.tt(m_t[k][:, :], x_t[k][:, :], rep_a[:, :], OP.mult)
              P.step()
              P.tt(m_t[k][:, :], m_t[k][:, :], rep_b[:, :], OP.add)
              P.step()
              P.dma("sync", a["xn1T"][k * 128:(k + 1) * 128, sl], m_t[k][:, :])
              if "agin" in a:
                  P.copy(agbf[:, :], m_t[k][:, :])
                  P.step()
                  P.dma("gpsimd", a["agin"][k * 128:(k + 1) * 128, sl], agbf[:, :])
              P.step()


def build_glue0():
    nc, a = _mk_nc(
        "glue0",
        ins={"moeT": ((D, T), F32), "xT": ((D, T), F32), "sT": ((D, T), F32),
             "lng": ((D, 1), F32), "lnb": ((D, 1), F32)},
        outs={"x1T": ((D, T), F32), "xn1T": ((D, T), F32)},
    )
    P = Prog(nc)
    stage_glue0(P, a)
    P.finish()
    return nc


def stage_l1attn(P, a):
  fused = "xnp2" in a
  with P.scope():
      CHA = 512
      win = [P.sb(f"win{k}", [128, 3 * D], BF16) for k in range(8)]
      wout = [P.sb(f"wout{k}", [128, D], BF16) for k in range(8)]
      stage = P.sb("stage", [128, 3 * D], F32)
      xnbf = [P.sb(f"xnbf{k}", [128, CHA], BF16) for k in range(8)]
      qkvbf = [P.sb(f"qkv{m}", [128, CHA], BF16) for m in range(24)]
      oT = [P.sb(f"oT{k}", [128, CHA], BF16) for k in range(8)]
      maskrep = P.sb("maskrep", [128, 512], F32)
      identbf = P.sb("identbf", [128, 128], BF16)
      ssb = P.sb("ssb", [128, 512], F32)
      Eb = P.sb("Eb", [128, 512], F32)
      nm = P.sb("nm", [128, 4], F32)
      sm = P.sb("sm", [128, 4], F32)
      Pbf = P.sb("Pbf", [128, 512], BF16)
      Ptb = P.sb("Ptb", [128, 512], BF16)
      vnb = P.sb("vnb", [128, 256], BF16)
      sout = P.sb("sout", [128, 512], F32)
      sout_bf = P.sb("sout_bf", [128, 512], BF16)
      pa = P.ps("pa", [128, 512])
      ps_s = P.ps("ps_s", [128, 512])
      ps_t = P.ps("ps_t", [128, 512], BF16)
      ps_vn = P.ps("ps_vn", [128, 256], BF16)
      ps_o = P.ps("ps_o", [64, 512])

      P.dma("sync", maskrep[:, :], a["maskrep"][:, :])
      P.dma("sync", identbf[:, :], a["identbf"][:, :])
      load_wbf(P, a["WinT"], 8, 3 * D, stage, win)
      load_wbf(P, a["WoutT"], 8, D, stage, wout)

      for c in range(T // CHA):
          sl = slice(c * CHA, c * CHA + CHA)
          if fused:
              pos0 = c * 64
              for k in range(8):
                  dst4 = xnbf[k][:, :].rearrange("p (g b q) -> p g b q", b=8, q=16)
                  for b in range(8):
                      src3 = a["xnp2"][b, k * 128:(k + 1) * 128, pos0:pos0 + 64]
                      src3 = src3.rearrange("p (g q) -> p g q", q=16)
                      P.dma("sync" if b % 2 == 0 else "gpsimd", dst4[:, :, b, :], src3)
              P.step()
          else:
              for k in range(8):
                  P.dma("sync", stage[:, 0:CHA], a["xnp"][k * 128:(k + 1) * 128, sl])
                  P.step()
                  P.copy(xnbf[k][:, :], stage[:, 0:CHA])
                  P.step()
          for mt in range(24):
              for k in range(8):
                  P.mm(pa[:, :], win[k][:, mt * 128:(mt + 1) * 128], xnbf[k][:, :],
                       start=(k == 0), stop=(k == 7))
              P.step()
              if mt < 8:
                  P.act(qkvbf[mt][:, :], pa[:, :], AF.Copy, scale=0.125)
              else:
                  P.copy(qkvbf[mt][:, :], pa[:, :])
              P.step()
          for blk in range(4):
              bs = slice(blk * 128, (blk + 1) * 128)
              for w4 in range(4):
                  hs = [[0, 2, 4, 6], [8, 10, 12, 14], [1, 3, 5, 7], [9, 11, 13, 15]][w4]
                  for i, h in enumerate(hs):
                      ro = (h % 2) * 64
                      P.mm(ps_s[:, i * 128:(i + 1) * 128],
                           qkvbf[h // 2][ro:ro + 64, bs],
                           qkvbf[8 + h // 2][ro:ro + 64, bs], start=True, stop=True)
                  P.step()
                  P.tt(ssb[:, :], ps_s[:, :], maskrep[:, :], OP.add)
                  P.step()
                  for i in range(4):
                      P.red(nm[:, i:i + 1], ssb[:, i * 128:(i + 1) * 128], OP.max, negate=True)
                  P.step()
                  for i in range(4):
                      P.act(Eb[:, i * 128:(i + 1) * 128], ssb[:, i * 128:(i + 1) * 128],
                            AF.Exp, bias=nm[:, i:i + 1])
                  P.step()
                  for i in range(4):
                      P.red(sm[:, i:i + 1], Eb[:, i * 128:(i + 1) * 128], OP.add)
                  P.step()
                  P.act(sm[:, :], sm[:, :], AF.Ln)
                  P.step()
                  P.act(sm[:, :], sm[:, :], AF.Exp, scale=-1.0)
                  P.step()
                  for i in range(4):
                      P.ts(Pbf[:, i * 128:(i + 1) * 128], Eb[:, i * 128:(i + 1) * 128],
                           sm[:, i:i + 1], OP.mult)
                  P.step()
                  for i, h in enumerate(hs):
                      P.transpose(ps_t[:, i * 128:(i + 1) * 128], Pbf[:, i * 128:(i + 1) * 128],
                                  identbf[:, :])
                  P.step()
                  for i, h in enumerate(hs):
                      ro = (h % 2) * 64
                      P.transpose(ps_vn[:, i * 64:(i + 1) * 64],
                                  qkvbf[16 + h // 2][ro:ro + 64, bs],
                                  identbf[ro:ro + 64, ro:ro + 64])
                  P.step()
                  P.copy(Ptb[:, :], ps_t[:, :])
                  P.copy(vnb[:, :], ps_vn[:, :], eng="vector")
                  P.step()
                  for i in range(4):
                      P.mm(ps_o[0:64, i * 128:(i + 1) * 128], vnb[:, i * 64:(i + 1) * 64],
                           Ptb[:, i * 128:(i + 1) * 128], start=True, stop=True)
                  P.step()
                  for i, h in enumerate(hs):
                      ro = (h % 2) * 64
                      P.copy(oT[h // 2][ro:ro + 64, bs], ps_o[0:64, i * 128:(i + 1) * 128])
                  P.step()
          for ot in range(8):
              for k in range(8):
                  P.mm(pa[:, :], wout[k][:, ot * 128:(ot + 1) * 128], oT[k][:, :],
                       start=(k == 0), stop=(k == 7))
              P.step()
              if fused:
                  P.copy(sout_bf[:, :], pa[:, :])
                  P.step()
                  pos0 = c * 64
                  src4 = sout_bf[:, :].rearrange("p (g b q) -> p g b q", b=8, q=16)
                  for b in range(8):
                      dst3 = a["s1loc"][b * D + ot * 128: b * D + (ot + 1) * 128,
                                        pos0:pos0 + 64]
                      dst3 = dst3.rearrange("p (g q) -> p g q", q=16)
                      P.dma("sync" if b % 2 == 0 else "gpsimd", dst3, src4[:, :, b, :])
                  P.step()
              else:
                  P.copy(sout[:, :], pa[:, :], eng="vector")
                  P.step()
                  P.dma("sync", a["s1p"][ot * 128:(ot + 1) * 128, sl], sout[:, :])
                  P.step()


def build_l1attn():
    nc, a = _mk_nc(
        "l1attn",
        ins={"xnp": ((D, T), F32), "WinT": ((D, 3 * D), F32), "WoutT": ((D, D), F32),
             "maskrep": ((128, 512), F32), "identbf": ((128, 128), BF16)},
        outs={"s1p": ((D, T), F32)},
    )
    P = Prog(nc)
    stage_l1attn(P, a)
    P.finish()
    return nc


def stage_l1fa(P, a):
  with P.scope():
      CHF = 512
      wqkv = [P.sb(f"wqkv{k}", [128, 1536], BF16) for k in range(8)]
      wo = [P.sb(f"wo{k}", [128, D], BF16) for k in range(4)]
      stage = P.sb("stage", [128, 1536], F32)
      permT = P.sb("permT", [128, 128], BF16)
      selq = P.sb("selq", [2, 128], F32)
      selk = P.sb("selk", [2, 128], F32)
      blko = P.sb("blko", [128, 2], F32)
      mp = [P.sb(f"mp{i}", [128, 512], F32) for i in range(4)]
      mc = [P.sb(f"mc{i}", [128, 512], F32) for i in range(4)]
      tabs = {n: P.sb(f"tab{n}", [128, CHF], F32) for n in ("qc", "qs", "kc", "ks")}
      x_t = [P.sb(f"x{k}", [128, CHF], F32) for k in range(8)]
      xs_t = [P.sb(f"xs{k}", [128, CHF], F32) for k in range(4)]
      h_bf = [P.sb(f"hbf{k}", [128, CHF], BF16) for k in range(8)]
      qT = [P.sb(f"qT{k}", [128, CHF], BF16) for k in range(4)]
      kT = [[P.sb(f"kT{p}_{k}", [128, CHF], BF16) for k in range(4)] for p in range(2)]
      vN = [[P.sb(f"vN{p}_{k}", [128, CHF], BF16) for k in range(4)] for p in range(2)]
      est = [P.sb(f"est{i}", [128, CHF], BF16) for i in range(8)]
      oT = [P.sb(f"oTf{k}", [128, CHF], BF16) for k in range(4)]
      sq = P.sb("sq", [128, CHF], F32)
      t2 = P.sb("t2", [128, CHF], F32)
      qn = P.sb("qn", [128, CHF], F32)
      qn_bf = P.sb("qn_bf", [128, CHF], BF16)
      ssrow = P.sb("ssrow", [2, 512], F32)
      invrow = P.sb("invrow", [1, 512], F32)
      repf = P.sb("repf", [128, 512], F32)
      repbf = P.sb("repbf", [128, 512], BF16)
      xmout = P.sb("xmout", [128, CHF], F32)
      ones_col_bf = P.sb("onescolbf", [128, 1], BF16)
      ones_row = P.sb("onesrow", [1, 128], F32)
      pm = P.ps("pm", [128, 512])
      pst = P.ps("pst", [128, 512])
      prep = P.ps("prep", [128, 512])
      prot = P.ps("prot", [128, 512])
      po = P.ps("po", [64, 512])

      P.memset(ones_col_bf[:, :], 1.0)
      P.memset(ones_row[:, :], 1.0)
      P.memset(blko[:, :], 0.0)
      P.memset(selq[:, :], 0.0)
      P.memset(selk[:, :], 0.0)
      P.step()
      P.memset(blko[0:64, 0:1], 1.0)
      P.step()
      P.memset(blko[64:128, 1:2], 1.0)
      P.step()
      P.dma("sync", selq[0:1, 0:64], a["qsv"][0:1, :])
      P.dma("sync", selk[0:1, 0:64], a["ksv"][0:1, :])
      P.step()
      P.dma("sync", selq[1:2, 64:128], a["qsv"][0:1, :])
      P.dma("sync", selk[1:2, 64:128], a["ksv"][0:1, :])
      P.dma("sync", permT[:, :], a["permT"][:, :])
      for i in range(4):
          P.dma("gpsimd", mp[i][:, :], a["mprev"][i * 128:(i + 1) * 128, :])
          P.dma("gpsimd", mc[i][:, :], a["mcur"][i * 128:(i + 1) * 128, :])
      P.step()
      load_wbf(P, a["wqkv"], 8, 1536, stage, wqkv)
      load_wbf(P, a["wo"], 4, D, stage, wo)

      for c in range(T // CHF):
          c0 = c * CHF
          sl = slice(c0, c0 + CHF)
          par = c % 2
          for k in range(8):
              P.dma("sync", x_t[k][:, :], a["xinT"][k * 128:(k + 1) * 128, sl])
          for n, t in tabs.items():
              P.dma("scalar", t[:, :], a["t" + n][:, sl])
          if c == 0:
              for k in range(4):
                  P.memset(xs_t[k][:, 0:1], 0.0, eng="gpsimd")
                  P.dma("gpsimd", xs_t[k][:, 1:CHF],
                        a["xinT"][(k + 4) * 128:(k + 5) * 128, 0:CHF - 1])
          else:
              for k in range(4):
                  P.dma("gpsimd", xs_t[k][:, :],
                        a["xinT"][(k + 4) * 128:(k + 5) * 128, c0 - 1:c0 + CHF - 1])
          P.step()
          for k in range(4):
              P.copy(h_bf[k][:, :], x_t[k][:, :])
              P.copy(h_bf[k + 4][:, :], xs_t[k][:, :], eng="vector")
          P.step()
          for mt in range(8):
              isq = mt < 4
              for k in range(8):
                  P.mm(pm[:, :], wqkv[k][:, mt * 128:(mt + 1) * 128], h_bf[k][:, :],
                       start=(k == 0), stop=(k == 7))
              P.step()
              P.act(sq[:, :], pm[:, :], AF.Square)
              P.step()
              P.mm(prep[0:2, :], blko[:, :], sq[:, :], start=True, stop=True)
              P.step()
              P.copy(ssrow[:, :], prep[0:2, :], eng="vector")
              P.step()
              P.act(ssrow[:, :], ssrow[:, :], AF.Ln)
              P.step()
              P.act(ssrow[:, :], ssrow[:, :], AF.Exp, scale=-0.5)
              P.step()
              P.ts(ssrow[:, :], ssrow[:, :], 1e12, OP.min)
              P.step()
              P.mm(prep[:, :], selq[:, :] if isq else selk[:, :], ssrow[:, :],
                   start=True, stop=True)
              P.step()
              P.copy(repf[:, :], prep[:, :], eng="vector")
              P.step()
              P.tt(qn[:, :], pm[:, :], repf[:, :], OP.mult)
              P.step()
              P.copy(qn_bf[:, :], qn[:, :])
              P.step()
              P.mm(prot[:, :], permT[:, :], qn_bf[:, :], start=True, stop=True)
              P.step()
              P.tt(t2[:, :], qn[:, :], tabs["qc" if isq else "kc"][:, :], OP.mult)
              P.step()
              P.tt(sq[:, :], prot[:, :], tabs["qs" if isq else "ks"][:, :], OP.mult)
              P.step()
              dst = qT[mt] if isq else kT[par][mt - 4]
              P.tt(dst[:, :], t2[:, :], sq[:, :], OP.add)
              P.step()
          for tt_ in range(4):
              for k in range(8):
                  P.mm(pm[:, :], h_bf[k][:, tt_ * 128:(tt_ + 1) * 128],
                       wqkv[k][:, 1024:1536], start=(k == 0), stop=(k == 7))
              P.step()
              P.copy(vN[par][tt_][:, :], pm[:, :])
              P.step()
          jts = ([] if c == 0 else [(1 - par, jb, mp[jb]) for jb in range(4)]) + \
                [(par, jb, mc[jb]) for jb in range(4)]
          for h in range(8):
              ro = (h % 2) * 64
              for ji, (p_, jb, msk) in enumerate(jts):
                  P.mm(pst[:, :], kT[p_][h // 2][ro:ro + 64, jb * 128:(jb + 1) * 128],
                       qT[h // 2][ro:ro + 64, :], start=True, stop=True)
                  P.step()
                  P.tt(pst[:, :], pst[:, :], msk[:, :], OP.add)
                  P.step()
                  P.act(est[ji][:, :], pst[:, :], AF.Exp)
                  P.step()
              for ji in range(len(jts)):
                  P.mm(prep[0:1, :], ones_col_bf[:, :], est[ji][:, :],
                       start=(ji == 0), stop=(ji == len(jts) - 1))
              P.step()
              P.copy(invrow[:, :], prep[0:1, :], eng="vector")
              P.step()
              P.act(invrow[:, :], invrow[:, :], AF.Ln)
              P.step()
              P.act(invrow[:, :], invrow[:, :], AF.Exp, scale=-1.0)
              P.step()
              P.mm(prep[:, :], ones_row[0:1, :], invrow[0:1, :], start=True, stop=True)
              P.step()
              P.copy(repbf[:, :], prep[:, :])
              P.step()
              for ji in range(len(jts)):
                  P.tt(est[ji][:, :], est[ji][:, :], repbf[:, :], OP.mult,
                       eng=("vector" if ji % 2 == 0 else "gpsimd"))
              P.step()
              for ji, (p_, jb, msk) in enumerate(jts):
                  P.mm(po[0:64, :], vN[p_][jb][:, h * 64:(h + 1) * 64], est[ji][:, :],
                       start=(ji == 0), stop=(ji == len(jts) - 1))
              P.step()
              P.copy(oT[h // 2][ro:ro + 64, :], po[0:64, :])
              P.step()
          for ot in range(8):
              for k in range(4):
                  P.mm(pm[:, :], wo[k][:, ot * 128:(ot + 1) * 128], oT[k][:, :],
                       start=(k == 0), stop=(k == 3))
              P.step()
              P.tt(xmout[:, :], pm[:, :], x_t[ot][:, :], OP.add)
              P.step()
              P.dma("sync", a["xmidT"][ot * 128:(ot + 1) * 128, sl], xmout[:, :])
              P.step()


def build_l1fa():
    nc, a = _mk_nc(
        "l1fa",
        ins={"xinT": ((D, T), F32), "wqkv": ((D, 1536), F32), "wo": ((512, D), F32),
             "qsv": ((1, 64), F32), "ksv": ((1, 64), F32),
             "tqc": ((128, T), F32), "tqs": ((128, T), F32),
             "tkc": ((128, T), F32), "tks": ((128, T), F32),
             "mprev": ((512, 512), F32), "mcur": ((512, 512), F32),
             "permT": ((128, 128), BF16)},
        outs={"xmidT": ((D, T), F32)},
    )
    P = Prog(nc)
    stage_l1fa(P, a)
    P.finish()
    return nc


def stage_l1fb1(P, a):
  with P.scope():
      CHF = 512
      NMT = 22
      w1a = [P.sb(f"w1a{k}", [128, 2730], BF16) for k in range(8)]
      w1g = [P.sb(f"w1g{k}", [128, 2730], BF16) for k in range(8)]
      stage = P.sb("stage", [128, 2730], F32)
      lng_t = [P.sb(f"lng{k}", [128, 1], F32) for k in range(8)]
      lnb_t = [P.sb(f"lnb{k}", [128, 1], F32) for k in range(8)]
      b1a_t = [P.sb(f"b1a{k}", [128, 1], F32) for k in range(NMT)]
      b1g_t = [P.sb(f"b1g{k}", [128, 1], F32) for k in range(NMT)]
      ones_col = P.sb("onescol", [128, 1], F32)
      ones_row = P.sb("onesrow", [1, 128], F32)
      xh = [P.sb(f"xh{k}", [128, CHF], F32) for k in range(8)]
      hn_bf = [P.sb(f"hnbf{k}", [128, CHF], BF16) for k in range(8)]
      sq = P.sb("sq", [128, CHF], F32)
      t2 = P.sb("t2", [128, CHF], F32)
      hebuf = P.sb("hebuf", [128, CHF], BF16)
      rep_a = P.sb("repa", [128, CHF], F32)
      rep_b = P.sb("repb", [128, CHF], F32)
      rstd_row = P.sb("rstd", [1, CHF], F32)
      nm_row = P.sb("nm", [1, CHF], F32)
      pa = P.ps("pa", [128, 512])
      pg = P.ps("pg", [128, 512])
      pr = P.ps("pr", [128, 512])

      P.memset(ones_col[:, :], 1.0)
      P.memset(ones_row[:, :], 1.0)
      load_pvec(P, a["lng"], lng_t)
      load_pvec(P, a["lnb"], lnb_t)
      for mt in range(NMT):
          m = 128 if mt < NMT - 1 else 42
          P.dma("sync", b1a_t[mt][0:m, :], a["b1a"][mt * 128:mt * 128 + m, :])
          P.dma("gpsimd", b1g_t[mt][0:m, :], a["b1g"][mt * 128:mt * 128 + m, :])
      P.step()
      load_wbf(P, a["w1a"], 8, 2730, stage, w1a)
      load_wbf(P, a["w1g"], 8, 2730, stage, w1g)

      for c in range(T // CHF):
          c0 = c * CHF
          sl = slice(c0, c0 + CHF)
          for k in range(4):
              P.dma("sync", xh[k][:, :], a["xmidT"][k * 128:(k + 1) * 128, sl])
          if c == 0:
              for k in range(4):
                  P.memset(xh[k + 4][:, 0:1], 0.0, eng="gpsimd")
                  P.dma("gpsimd", xh[k + 4][:, 1:CHF],
                        a["xmidT"][(k + 4) * 128:(k + 5) * 128, 0:CHF - 1])
          else:
              for k in range(4):
                  P.dma("gpsimd", xh[k + 4][:, :],
                        a["xmidT"][(k + 4) * 128:(k + 5) * 128, c0 - 1:c0 + CHF - 1])
          P.step()
          ln_rows(P, xh, CHF, sq, pa, ones_col, ones_row, rstd_row, nm_row,
                  rep_a, rep_b, 1e-5, pr)
          for k in range(8):
              P.tt(xh[k][:, :], xh[k][:, :], rep_a[:, :], OP.mult)
              P.step()
              P.tt(xh[k][:, :], xh[k][:, :], rep_b[:, :], OP.add)
              P.step()
              P.ts2(xh[k][:, :], xh[k][:, :], lng_t[k][:, 0:1], lnb_t[k][:, 0:1],
                    OP.mult, OP.add)
              P.step()
              P.copy(hn_bf[k][:, :], xh[k][:, :])
              P.step()
          for mt in range(NMT):
              m = 128 if mt < NMT - 1 else 42
              msl = slice(mt * 128, mt * 128 + m)
              for k in range(8):
                  P.mm(pa[0:m, :], w1a[k][:, msl], hn_bf[k][:, :],
                       start=(k == 0), stop=(k == 7))
                  P.mm(pg[0:m, :], w1g[k][:, msl], hn_bf[k][:, :],
                       start=(k == 0), stop=(k == 7))
              P.step()
              P.act(sq[0:m, :], pg[0:m, :], AF.Gelu, bias=b1g_t[mt][0:m, 0:1])
              P.step()
              P.ts(t2[0:m, :], pa[0:m, :], b1a_t[mt][0:m, 0:1], OP.add)
              P.step()
              P.tt(hebuf[0:m, :], t2[0:m, :], sq[0:m, :], OP.mult)
              P.step()
              P.dma("sync", a["heT"][mt * 128:mt * 128 + m, sl], hebuf[0:m, :])
              P.step()


def build_l1fb1():
    nc, a = _mk_nc(
        "l1fb1",
        ins={"xmidT": ((D, T), F32), "lng": ((D, 1), F32), "lnb": ((D, 1), F32),
             "w1a": ((D, 2730), F32), "w1g": ((D, 2730), F32),
             "b1a": ((2730, 1), F32), "b1g": ((2730, 1), F32)},
        outs={"heT": ((2816, T), BF16)},
    )
    P = Prog(nc)
    stage_l1fb1(P, a)
    P.finish()
    return nc


def stage_l1fb2(P, a, final):
  s1bf16 = "s1bfT" in a
  with P.scope():
      CHF = 512
      s1b = P.sb("s1b", [128, 512], BF16) if s1bf16 else None
      NMT = 22
      w2 = [P.sb(f"w2{k}", [128, D], BF16) for k in range(NMT)]
      stage = P.sb("stage", [128, D], F32)
      b2_t = [P.sb(f"b2{k}", [128, 1], F32) for k in range(8)]
      he_t = [P.sb(f"he{k}", [128, CHF], BF16) for k in range(NMT)]
      xm = P.sb("xm", [128, CHF], F32)
      x1 = P.sb("x1", [128, CHF], F32)
      s1 = P.sb("s1", [128, CHF], F32)
      t2 = P.sb("t2", [128, CHF], F32)
      yb = P.sb("yb", [128, CHF], F32)
      pa = P.ps("pa", [128, 512])

      load_pvec(P, a["b2"], b2_t)
      for kt in range(NMT):
          P.dma("sync", stage[:, :], a["w2"][kt * 128:(kt + 1) * 128, :])
          P.step()
          P.copy(w2[kt][:, :], stage[:, :])
          P.step()
      for c in range(T // CHF):
          sl = slice(c * CHF, c * CHF + CHF)
          for kt in range(NMT):
              P.dma("sync", he_t[kt][:, :], a["heT"][kt * 128:(kt + 1) * 128, sl])
          P.step()
          for ot in range(8):
              for kt in range(NMT):
                  P.mm(pa[:, :], w2[kt][:, ot * 128:(ot + 1) * 128], he_t[kt][:, :],
                       start=(kt == 0), stop=(kt == NMT - 1))
              P.dma("gpsimd", xm[:, :], a["xmidT"][ot * 128:(ot + 1) * 128, sl])
              if final:
                  P.dma("scalar", x1[:, :], a["x1T"][ot * 128:(ot + 1) * 128, sl])
                  if s1bf16:
                      P.dma("sync", s1b[:, :], a["s1bfT"][ot * 128:(ot + 1) * 128, sl])
                  else:
                      P.dma("sync", s1[:, :], a["s1T"][ot * 128:(ot + 1) * 128, sl])
              P.step()
              if final and s1bf16:
                  P.copy(s1[:, :], s1b[:, :])
                  P.step()
              P.ts(yb[:, :], pa[:, :], b2_t[ot][:, 0:1], OP.add)
              P.step()
              P.tt(yb[:, :], yb[:, :], xm[:, :], OP.add)
              P.step()
              if final:
                  P.act(yb[:, :], yb[:, :], AF.Silu)
                  P.step()
                  P.tt(yb[:, :], yb[:, :], s1[:, :], OP.mult)
                  P.step()
                  P.tt(yb[:, :], yb[:, :], x1[:, :], OP.add)
                  P.step()
              P.dma("sync", a["yT"][ot * 128:(ot + 1) * 128, sl], yb[:, :])
              P.step()


def build_l1fb2(final):
    ins = {"heT": ((2816, T), BF16), "xmidT": ((D, T), F32),
           "w2": ((2816, D), F32), "b2": ((D, 1), F32)}
    if final:
        ins.update({"x1T": ((D, T), F32), "s1T": ((D, T), F32)})
    nc, a = _mk_nc("l1fb2", ins=ins, outs={"yT": ((D, T), F32)})
    P = Prog(nc)
    stage_l1fb2(P, a, final)
    P.finish()
    return nc


def build_fused():
    ins = {
        "xT": ((D, T), F32), "WzT": ((D, D), F32), "WzaT": ((D, D), F32),
        "bz": ((D, 1), F32), "bza": ((D, 1), F32),
        "lc_re": ((D, 1), F32), "lc_im": ((D, 1), F32),
        "lam_a": ((D, 1), F32), "lam_b": ((D, 1), F32),
        "tabP": ((D, T), F32), "tabQ": ((D, T), F32),
        "tabU": ((D, T), F32), "tabV": ((D, T), F32),
        "WyT": ((D, D), F32), "WyaT": ((D, D), F32),
        "by": ((D, 1), F32), "bya": ((D, 1), F32), "norm_w": ((D, 1), F32),
        "gate_wT": ((D, 8), F32), "ew1": ((8, D, 2048), F32),
        "ew2": ((8, 2048, D), F32), "identf": ((128, 128), F32),
        "lngm": ((D, 1), F32), "lnbm": ((D, 1), F32),
        "WinT": ((D, 3 * D), F32), "WoutT": ((D, D), F32),
        "maskrep": ((128, 512), F32), "identbf": ((128, 128), BF16),
        "tqc": ((128, T), F32), "tqs": ((128, T), F32),
        "tkc": ((128, T), F32), "tks": ((128, T), F32),
        "mprev": ((512, 512), F32), "mcur": ((512, 512), F32),
        "permT": ((128, 128), BF16),
    }
    for l in range(2):
        ins.update({
            f"wqkv{l}": ((D, 1536), F32), f"wo{l}": ((512, D), F32),
            f"qsv{l}": ((1, 64), F32), f"ksv{l}": ((1, 64), F32),
            f"lng{l}": ((D, 1), F32), f"lnb{l}": ((D, 1), F32),
            f"w1a{l}": ((D, 2730), F32), f"w1g{l}": ((D, 2730), F32),
            f"b1a{l}": ((2730, 1), F32), f"b1g{l}": ((2730, 1), F32),
            f"w2{l}": ((2816, D), F32), f"b2{l}": ((D, 1), F32),
        })
    nc, a = _mk_nc("fused", ins=ins, outs={"outT": ((D, T), F32)})
    it = lambda n, shp, dt: nc.dram_tensor(n, list(shp), dt, kind="Internal").ap()
    xnT = it("xnT", (D, T), F32)
    xnbf_i = it("xnbf_i", (D, T), BF16)
    hT = it("hT", (D, T), F32)
    sT = it("sT", (D, T), F32)
    moeT = it("moeT", (D, T), F32)
    x1T = it("x1T", (D, T), F32)
    xn1T = it("xn1T", (D, T), F32)
    agin = it("agin", (D, T), BF16)
    agout1 = it("agout1", (8 * D, T), BF16)
    xnp2 = it("xnp2", (8, D, 1024), BF16)
    s1loc = it("s1loc", (8 * D, 1024), BF16)
    agout2 = it("agout2", (8 * 8 * D, 1024), BF16)
    s1own = it("s1own", (D, T), BF16)
    xmidT = it("xmidT", (D, T), F32)
    heT = it("heT", (2816, T), BF16)
    y0T = it("y0T", (D, T), F32)
    w1s = it("w1s", (8, D, 2048), BF16)
    w2s = it("w2s", (8, 2048, D), BF16)

    P = Prog(nc)
    ag1_sem = P.st.enter_context(nc.semaphore("ag1"))
    ag2_sem = P.st.enter_context(nc.semaphore("ag2"))

    stage_l0h(P, {"xT": a["xT"], "WzT": a["WzT"], "WzaT": a["WzaT"],
                  "bz": a["bz"], "bza": a["bza"], "lc_re": a["lc_re"],
                  "lc_im": a["lc_im"], "lam_a": a["lam_a"], "lam_b": a["lam_b"],
                  "tabP": a["tabP"], "tabQ": a["tabQ"], "tabU": a["tabU"],
                  "tabV": a["tabV"], "xnT": xnT, "hT": hT, "xnbfT": xnbf_i})
    stage_l0y(P, {"hT": hT, "xnT": xnT, "WyT": a["WyT"], "WyaT": a["WyaT"],
                  "by": a["by"], "bya": a["bya"], "norm_w": a["norm_w"], "sT": sT})
    stage_l0moe(P, {"xnT": xnT, "xnbfT": xnbf_i, "gate_wT": a["gate_wT"],
                    "ew1": a["ew1"], "ew2": a["ew2"], "identf": a["identf"],
                    "moeT": moeT, "w1s": w1s, "w2s": w2s})
    stage_glue0(P, {"moeT": moeT, "xT": a["xT"], "sT": sT, "lng": a["lngm"],
                    "lnb": a["lnbm"], "x1T": x1T, "xn1T": xn1T, "agin": agin})
    P._pre("gpsimd")
    inst = P.eng("gpsimd").collective_compute(
        "AllGather", OP.bypass, ins=[agin[:, :]], outs=[agout1[:, :]],
        replica_groups=[list(range(8))])
    inst.then_inc(ag1_sem, 16)
    if "gpsimd" not in P.active:
        P.active["gpsimd"] = None
    P.step()
    stage_l1fa(P, {"xinT": xn1T, "wqkv": a["wqkv0"], "wo": a["wo0"],
                   "qsv": a["qsv0"], "ksv": a["ksv0"], "tqc": a["tqc"],
                   "tqs": a["tqs"], "tkc": a["tkc"], "tks": a["tks"],
                   "mprev": a["mprev"], "mcur": a["mcur"], "permT": a["permT"],
                   "xmidT": xmidT})
    for e in ENGS:
        P.eng(e).wait_ge(ag1_sem, 16)
    P._pre("sync")
    for arm in P.nc.sync.switch_core_id(n=128):
        lg = arm.logical
        if 0 <= lg < 8:
            for b in range(8):
                i2 = P.nc.sync.dma_start(
                    out=xnp2[b, :, :],
                    in_=agout1[b * D:(b + 1) * D, lg * 1024:(lg + 1) * 1024])
                i2.then_inc(P.dma_sem, 16)
    P.step_dmas += 8
    if "sync" not in P.active:
        P.active["sync"] = None
    P.step()
    stage_l1attn(P, {"xnp2": xnp2, "s1loc": s1loc, "WinT": a["WinT"],
                     "WoutT": a["WoutT"], "maskrep": a["maskrep"],
                     "identbf": a["identbf"]})
    P._pre("gpsimd")
    inst = P.eng("gpsimd").collective_compute(
        "AllGather", OP.bypass, ins=[s1loc[:, :]], outs=[agout2[:, :]],
        replica_groups=[list(range(8))])
    inst.then_inc(ag2_sem, 16)
    if "gpsimd" not in P.active:
        P.active["gpsimd"] = None
    P.step()
    stage_l1fb1(P, {"xmidT": xmidT, "lng": a["lng0"], "lnb": a["lnb0"],
                    "w1a": a["w1a0"], "w1g": a["w1g0"], "b1a": a["b1a0"],
                    "b1g": a["b1g0"], "heT": heT})
    stage_l1fb2(P, {"heT": heT, "xmidT": xmidT, "w2": a["w20"], "b2": a["b20"],
                    "yT": y0T}, final=False)
    stage_l1fa(P, {"xinT": y0T, "wqkv": a["wqkv1"], "wo": a["wo1"],
                   "qsv": a["qsv1"], "ksv": a["ksv1"], "tqc": a["tqc"],
                   "tqs": a["tqs"], "tkc": a["tkc"], "tks": a["tks"],
                   "mprev": a["mprev"], "mcur": a["mcur"], "permT": a["permT"],
                   "xmidT": xmidT})
    stage_l1fb1(P, {"xmidT": xmidT, "lng": a["lng1"], "lnb": a["lnb1"],
                    "w1a": a["w1a1"], "w1g": a["w1g1"], "b1a": a["b1a1"],
                    "b1g": a["b1g1"], "heT": heT})
    for e in ENGS:
        P.eng(e).wait_ge(ag2_sem, 16)
    P._pre("sync")
    for arm in P.nc.sync.switch_core_id(n=128):
        lg = arm.logical
        if 0 <= lg < 8:
            for c in range(8):
                i2 = P.nc.sync.dma_start(
                    out=s1own[:, c * 1024:(c + 1) * 1024],
                    in_=agout2[c * 8 * D + lg * D:c * 8 * D + (lg + 1) * D, :])
                i2.then_inc(P.dma_sem, 16)
    P.step_dmas += 8
    if "sync" not in P.active:
        P.active["sync"] = None
    P.step()
    stage_l1fb2(P, {"heT": heT, "xmidT": xmidT, "w2": a["w21"], "b2": a["b21"],
                    "x1T": x1T, "s1bfT": s1own, "yT": a["outT"]}, final=True)
    P.finish()
    return nc


def kernel_fused(inp):
    f32 = np.float32
    asc = np.ascontiguousarray
    C = make_consts()
    x = np.asarray(inp["L0_attn_x"], f32)
    Win = np.asarray(inp["L1_attn_in_proj_weight"], f32)
    perm = np.array([h * 192 + w * 64 + dd for w in range(3) for h in range(16)
                     for dd in range(64)])
    r = np.arange(128)
    maskblk2 = np.where((r[:, None] % 16) == (r[None, :] % 16), 0.0, -1e30).astype(f32)
    w1 = np.asarray(inp["L1_ffn_w1"], f32)
    b1 = np.asarray(inp["L1_ffn_b1"], f32)
    w2in = np.asarray(inp["L1_ffn_w2"], f32)
    base = {
        "WzT": asc(np.asarray(inp["L0_attn_Wz"], f32).T),
        "WzaT": asc(np.asarray(inp["L0_attn_Wza"], f32).T),
        "bz": np.asarray(inp["L0_attn_bz"], f32).reshape(-1, 1),
        "bza": np.asarray(inp["L0_attn_bza"], f32).reshape(-1, 1),
        "lc_re": np.asarray(inp["L0_attn_lc_re"], f32).reshape(-1, 1),
        "lc_im": np.asarray(inp["L0_attn_lc_im"], f32).reshape(-1, 1),
        "lam_a": C["lam_a"], "lam_b": C["lam_b"],
        "tabP": C["tabP"], "tabQ": C["tabQ"], "tabU": C["tabU"], "tabV": C["tabV"],
        "WyT": asc(np.asarray(inp["L0_attn_Wy"], f32).T),
        "WyaT": asc(np.asarray(inp["L0_attn_Wya"], f32).T),
        "by": np.asarray(inp["L0_attn_by"], f32).reshape(-1, 1),
        "bya": np.asarray(inp["L0_attn_bya"], f32).reshape(-1, 1),
        "norm_w": np.asarray(inp["L0_attn_norm_w"], f32).reshape(-1, 1),
        "gate_wT": asc(np.asarray(inp["L0_ffn_gate_w"], f32).T),
        "ew1": np.asarray(inp["L0_ffn_expert_w1"], f32),
        "ew2": np.asarray(inp["L0_ffn_expert_w2"], f32),
        "identf": C["identf"], "identbf": C["identbf"],
        "lngm": np.asarray(inp["L0_ffn_ln_gamma"], f32).reshape(-1, 1),
        "lnbm": np.asarray(inp["L0_ffn_ln_beta"], f32).reshape(-1, 1),
        "WinT": asc(Win[perm, :].T),
        "WoutT": asc(np.asarray(inp["L1_attn_out_weight"], f32).T),
        "maskrep": np.tile(maskblk2, (1, 4)).astype(f32),
        "tqc": C["tab_qc"], "tqs": C["tab_qs"],
        "tkc": C["tab_kc"], "tks": C["tab_ks"],
        "mprev": C["maskTprev"], "mcur": C["maskTcur"], "permT": C["permT"],
    }
    for l in range(2):
        w2p = np.zeros((2816, D), f32)
        w2p[:2730] = w2in[l]
        base.update({
            f"wqkv{l}": asc(np.asarray(inp["L1_ffn_wqkv"], f32)[l]),
            f"wo{l}": asc(np.asarray(inp["L1_ffn_wo"], f32)[l]),
            f"qsv{l}": np.asarray(inp["L1_ffn_q_scale"], f32)[l].reshape(1, 64),
            f"ksv{l}": np.asarray(inp["L1_ffn_k_scale"], f32)[l].reshape(1, 64),
            f"lng{l}": np.asarray(inp["L1_ffn_ln_g"], f32)[l].reshape(-1, 1),
            f"lnb{l}": np.asarray(inp["L1_ffn_ln_b"], f32)[l].reshape(-1, 1),
            f"w1a{l}": asc(w1[l][:, :2730]), f"w1g{l}": asc(w1[l][:, 2730:]),
            f"b1a{l}": b1[l][:2730].reshape(-1, 1),
            f"b1g{l}": b1[l][2730:].reshape(-1, 1),
            f"w2{l}": w2p, f"b2{l}": np.asarray(inp["L1_ffn_b2"], f32)[l].reshape(-1, 1),
        })
    nc = build_fused()
    im = [dict(base, xT=asc(x[b].T)) for b in range(NB)]
    res = run_bass_kernel_spmd(nc, im, core_ids=list(range(NCORE)))
    return np.stack([asc(r["outT"].T) for r in res.results]).astype(f32)


USE_FUSED = False



def _spmd(nc, im):
    return run_bass_kernel_spmd(nc, im, core_ids=list(range(NCORE)))


def kernel(**inp):
    _ = (inp.get("L1_attn_key"), inp.get("L1_attn_value"))
    if USE_FUSED:
        return kernel_fused(inp)
    f32 = np.float32
    asc = np.ascontiguousarray
    C = make_consts()
    x = np.asarray(inp["L0_attn_x"], f32)
    xT = [asc(x[b].T) for b in range(NB)]

    base = {
        "WzT": asc(np.asarray(inp["L0_attn_Wz"], f32).T),
        "WzaT": asc(np.asarray(inp["L0_attn_Wza"], f32).T),
        "bz": np.asarray(inp["L0_attn_bz"], f32).reshape(-1, 1),
        "bza": np.asarray(inp["L0_attn_bza"], f32).reshape(-1, 1),
        "lc_re": np.asarray(inp["L0_attn_lc_re"], f32).reshape(-1, 1),
        "lc_im": np.asarray(inp["L0_attn_lc_im"], f32).reshape(-1, 1),
        "lam_a": C["lam_a"], "lam_b": C["lam_b"],
        "tabP": C["tabP"], "tabQ": C["tabQ"], "tabU": C["tabU"], "tabV": C["tabV"],
    }
    res = _spmd(build_l0h(), [dict(base, xT=xT[b]) for b in range(NB)])
    xnT = [r["xnT"] for r in res.results]
    hT = [r["hT"] for r in res.results]
    xnbfT = [r["xnbfT"] for r in res.results]

    base = {
        "WyT": asc(np.asarray(inp["L0_attn_Wy"], f32).T),
        "WyaT": asc(np.asarray(inp["L0_attn_Wya"], f32).T),
        "by": np.asarray(inp["L0_attn_by"], f32).reshape(-1, 1),
        "bya": np.asarray(inp["L0_attn_bya"], f32).reshape(-1, 1),
        "norm_w": np.asarray(inp["L0_attn_norm_w"], f32).reshape(-1, 1),
    }
    res = _spmd(build_l0y(), [dict(base, hT=hT[b], xnT=xnT[b]) for b in range(NB)])
    sT = [r["sT"] for r in res.results]

    base = {
        "gate_wT": asc(np.asarray(inp["L0_ffn_gate_w"], f32).T),
        "ew1": np.asarray(inp["L0_ffn_expert_w1"], f32),
        "ew2": np.asarray(inp["L0_ffn_expert_w2"], f32),
        "identf": C["identf"],
    }
    res = _spmd(build_l0moe(), [dict(base, xnT=xnT[b], xnbfT=xnbfT[b])
                                for b in range(NB)])
    moeT = [r["moeT"] for r in res.results]

    base = {"lng": np.asarray(inp["L0_ffn_ln_gamma"], f32).reshape(-1, 1),
            "lnb": np.asarray(inp["L0_ffn_ln_beta"], f32).reshape(-1, 1)}
    res = _spmd(build_glue0(), [dict(base, moeT=moeT[b], xT=xT[b], sT=sT[b])
                                for b in range(NB)])
    x1T = [r["x1T"] for r in res.results]
    xn1T = [r["xn1T"] for r in res.results]

    Win = np.asarray(inp["L1_attn_in_proj_weight"], f32)
    perm = np.array([h * 192 + w * 64 + dd for w in range(3) for h in range(16)
                     for dd in range(64)])
    base = {"WinT": asc(Win[perm, :].T),
            "WoutT": asc(np.asarray(inp["L1_attn_out_weight"], f32).T),
            "maskrep": np.tile(C["maskblk"], (1, 4)).astype(f32),
            "identbf": C["identbf"]}
    im = []
    for c in range(NCORE):
        ps = slice(c * 1024, (c + 1) * 1024)
        arr = np.stack([xn1T[b][:, ps] for b in range(NB)], axis=-1)
        im.append(dict(base, xnp=asc(arr.reshape(D, T))))
    res = _spmd(build_l1attn(), im)
    s1T = [np.empty((D, T), f32) for _ in range(NB)]
    for c in range(NCORE):
        arr = res.results[c]["s1p"].reshape(D, 1024, NB)
        for b in range(NB):
            s1T[b][:, c * 1024:(c + 1) * 1024] = arr[:, :, b]

    w1 = np.asarray(inp["L1_ffn_w1"], f32)
    b1 = np.asarray(inp["L1_ffn_b1"], f32)
    w2in = np.asarray(inp["L1_ffn_w2"], f32)
    cur = xn1T
    out = None
    for l in range(2):
        basea = {"wqkv": asc(np.asarray(inp["L1_ffn_wqkv"], f32)[l]),
                 "wo": asc(np.asarray(inp["L1_ffn_wo"], f32)[l]),
                 "qsv": np.asarray(inp["L1_ffn_q_scale"], f32)[l].reshape(1, 64),
                 "ksv": np.asarray(inp["L1_ffn_k_scale"], f32)[l].reshape(1, 64),
                 "tqc": C["tab_qc"], "tqs": C["tab_qs"],
                 "tkc": C["tab_kc"], "tks": C["tab_ks"],
                 "mprev": C["maskTprev"], "mcur": C["maskTcur"], "permT": C["permT"]}
        res = _spmd(build_l1fa(), [dict(basea, xinT=cur[b]) for b in range(NB)])
        xmT = [r["xmidT"] for r in res.results]
        baseb = {"lng": np.asarray(inp["L1_ffn_ln_g"], f32)[l].reshape(-1, 1),
                 "lnb": np.asarray(inp["L1_ffn_ln_b"], f32)[l].reshape(-1, 1),
                 "w1a": asc(w1[l][:, :2730]), "w1g": asc(w1[l][:, 2730:]),
                 "b1a": b1[l][:2730].reshape(-1, 1), "b1g": b1[l][2730:].reshape(-1, 1)}
        res = _spmd(build_l1fb1(), [dict(baseb, xmidT=xmT[b]) for b in range(NB)])
        heT = [r["heT"] for r in res.results]
        w2p = np.zeros((2816, D), f32)
        w2p[:2730] = w2in[l]
        basec = {"w2": w2p, "b2": np.asarray(inp["L1_ffn_b2"], f32)[l].reshape(-1, 1)}
        final = l == 1
        if final:
            im = [dict(basec, heT=heT[b], xmidT=xmT[b], x1T=x1T[b], s1T=s1T[b])
                  for b in range(NB)]
        else:
            im = [dict(basec, heT=heT[b], xmidT=xmT[b]) for b in range(NB)]
        res = _spmd(build_l1fb2(final=final), im)
        cur = [r["yT"] for r in res.results]
    return np.stack([asc(cur[b].T) for b in range(NB)]).astype(f32)


def build_smoke():
    nc, a = _mk_nc(
        "smoke",
        ins={"A": ((128, 256), F32), "W": ((128, 128), F32), "b": ((128, 1), F32),
             "ident": ((128, 128), F32), "identbf": ((128, 128), BF16)},
        outs={"mmout": ((128, 256), F32), "siluout": ((128, 256), F32),
              "rowmax": ((128, 1), F32), "cums": ((128, 256), F32),
              "tpose": ((128, 128), F32), "repl": ((128, 256), F32),
              "divout": ((128, 256), F32)},
    )
    P = Prog(nc)
    at = P.sb("at", [128, 256], F32)
    wt = P.sb("wt", [128, 128], F32)
    bt = P.sb("bt", [128, 1], F32)
    it_bf = P.sb("it_bf", [128, 128], BF16)
    abf = P.sb("abf", [128, 256], BF16)
    wbf = P.sb("wbf", [128, 128], BF16)
    ones = P.sb("ones", [128, 256], F32)
    onesrow = P.sb("onesrow", [1, 128], F32)
    res = P.sb("res", [128, 256], F32)
    res2 = P.sb("res2", [128, 256], F32)
    rmax = P.sb("rmax", [128, 1], F32)
    cums = P.sb("cums", [128, 256], F32)
    tps = P.sb("tps", [128, 128], F32)
    tpbf = P.sb("tpbf", [128, 128], BF16)
    repl = P.sb("repl", [128, 256], F32)
    dv = P.sb("dv", [128, 256], F32)
    pm = P.ps("pm", [128, 512], F32)
    ptp = P.ps("ptp", [128, 128], BF16)

    P.dma("sync", at[:, :], a["A"][:, :])
    P.dma("sync", wt[:, :], a["W"][:, :])
    P.dma("sync", bt[:, :], a["b"][:, :])
    P.dma("sync", it_bf[:, :], a["identbf"][:, :])
    P.memset(ones[:, :], 1.0)
    P.memset(onesrow[:, :], 1.0)
    P.step()
    P.copy(abf[:, :], at[:, :])
    P.copy(wbf[:, :], wt[:, :], eng="vector")
    P.step()
    P.mm(pm[:, 0:256], wbf[:, :], abf[:, :], start=True, stop=True)
    P.step()
    P.ts(res[:, :], pm[:, 0:256], bt[:, 0:1], OP.add)
    P.step()
    P.dma("sync", a["mmout"][:, :], res[:, :])
    P.act(res2[:, :], pm[:, 0:256], AF.Silu, bias=bt[:, 0:1], scale=2.0)
    P.step()
    P.dma("sync", a["siluout"][:, :], res2[:, :])
    P.red(rmax[:, :], at[:, :], OP.max, negate=True)
    P.scan_cumsum(cums[:, :], ones[:, :], at[:, :], 0.0)
    P.step()
    P.dma("sync", a["rowmax"][:, :], rmax[:, :])
    P.dma("sync", a["cums"][:, :], cums[:, :])
    P.transpose(ptp[:, :], abf[:, 0:128], it_bf[:, :])
    P.step()
    P.copy(tps[:, :], ptp[:, :])
    P.step()
    P.dma("sync", a["tpose"][:, :], tps[:, :])
    P.mm(pm[:, 256:512], onesrow[0:1, :], at[0:1, :], start=True, stop=True)
    P.step()
    P.copy(repl[:, :], pm[:, 256:512], eng="vector")
    P.step()
    P.act(dv[:, :], repl[:, :], AF.Abs)
    P.step()
    P.act(dv[:, :], dv[:, :], AF.Ln)
    P.step()
    P.act(dv[:, :], dv[:, :], AF.Exp, scale=-1.0)
    P.step()
    P.tt(dv[:, :], at[:, :], dv[:, :], OP.mult)
    P.step()
    P.dma("sync", a["repl"][:, :], repl[:, :])
    P.dma("sync", a["divout"][:, :], dv[:, :])
    P.finish()
    return nc


def run_smoke():
    rng = np.random.default_rng(0)
    A = rng.standard_normal((128, 256), dtype=np.float32)
    W = rng.standard_normal((128, 128), dtype=np.float32)
    b = rng.standard_normal((128, 1), dtype=np.float32)
    ident = np.eye(128, dtype=np.float32)
    import ml_dtypes
    identbf = ident.astype(ml_dtypes.bfloat16)
    nc = build_smoke()
    im = {"A": A, "W": W, "b": b, "ident": ident, "identbf": identbf}
    res = run_bass_kernel_spmd(nc, [im], core_ids=[0])
    r = res.results[0]
    Abf = A.astype(ml_dtypes.bfloat16).astype(np.float32)
    Wbf = W.astype(ml_dtypes.bfloat16).astype(np.float32)
    mm = Wbf.T @ Abf
    def cmp(name, got, want, tol=2e-2):
        err = np.abs(got - want).max() / (np.abs(want).max() + 1e-9)
        print(f"{name:10s} relmaxerr={err:.2e} {'OK' if err < tol else 'FAIL'}")
    cmp("mmout", r["mmout"], mm + b)
    cmp("siluout", r["siluout"], (2*(mm)+b) / (1+np.exp(-(2*mm+b))))
    cmp("rowmax", r["rowmax"], -A.max(1, keepdims=True))
    cmp("cums", r["cums"], np.cumsum(A, axis=1), tol=1e-5)
    cmp("tpose", r["tpose"], Abf[:, :128].T)
    cmp("repl", r["repl"], np.broadcast_to(A[0:1, :], (128, 256)))
    cmp("divout", r["divout"], A / np.abs(A[0:1, :]), tol=1e-4)


if __name__ == "__main__":
    run_smoke()
